# version 13 (final)
# speedup vs baseline: 1.0840x; 1.0175x over previous
import numpy as np
import ml_dtypes
from contextlib import ExitStack

import concourse.bass as bass
import concourse.mybir as mybir
from concourse.bass_utils import run_bass_kernel_spmd

F32 = mybir.dt.float32
BF16 = mybir.dt.bfloat16
I32 = mybir.dt.int32
I8 = mybir.dt.int8
AF = mybir.ActivationFunctionType
ALU = mybir.AluOpType
AX = mybir.AxisListType.X

NB16 = ml_dtypes.bfloat16

B, L, D = 4, 4096, 1024
DI, DS, RK = 2048, 16, 64
DHALF = 1024
TT = 512
NT_FULL = L // TT
KC = 8
PRE_STOP = 99


class Sem:
    def __init__(self, h):
        self.h = h
        self.n = 0

    def inc(self, k=1):
        self.n += k
        return self.n


class Ctx:
    ENGINES = ("sync", "scalar", "vector", "gpsimd", "tensor")

    def __init__(self):
        self.nc = bass.Bass("TRN2", target_bir_lowering=False)
        self.ops = {e: [] for e in self.ENGINES}
        self._es = ExitStack()
        self.all_sems = []
        self._scopes = []

    def push_scope(self):
        es = ExitStack()
        self._scopes.append(self._es)
        self._es = es

    def pop_scope(self):
        es = self._es
        self._es = self._scopes.pop()
        es.close()

    def dram(self, name, shape, dtype):
        return self.nc.dram_tensor(name, shape, dtype).ap()

    def join(self):
        for eng in self.ENGINES:
            for s in self.all_sems:
                if s.n > 0:
                    self.wait(eng, s, s.n)

    def collective(self, kind, op, in_t, out_t, sem):
        v = sem.inc(1)
        rg = [[0, 1], [2, 3], [4, 5], [6, 7]]

        def f(e, kind=kind, op=op, in_t=in_t, out_t=out_t, sem=sem, rg=rg):
            e.collective_compute(kind, op, replica_groups=rg,
                                 ins=[in_t.opt()], outs=[out_t.opt()]).then_inc(sem.h)
        self.ops["gpsimd"].append(f)
        return v

    def din(self, name, shape, dtype):
        return self.nc.dram_tensor(name, shape, dtype, kind="ExternalInput").ap()

    def dout(self, name, shape, dtype):
        return self.nc.dram_tensor(name, shape, dtype, kind="ExternalOutput").ap()

    _uid = 0

    def _unm(self, name):
        Ctx._uid += 1
        return f"{name}_u{Ctx._uid}"

    def sbuf(self, name, shape, dtype):
        return self._es.enter_context(self.nc.sbuf_tensor(self._unm(name), shape, dtype))

    def psum(self, name, shape, dtype=F32):
        return self._es.enter_context(self.nc.psum_tensor(self._unm(name), shape, dtype))

    def sem(self, name):
        if not hasattr(self, "_sem_cache"):
            self._sem_cache = {}
        if name in self._sem_cache:
            return self._sem_cache[name]
        nm = f"{name}_{len(self.all_sems)}"
        root = self._scopes[0] if self._scopes else self._es
        s = Sem(root.enter_context(self.nc.semaphore(nm)))
        self.all_sems.append(s)
        self._sem_cache[name] = s
        return s

    def op(self, eng, fn):
        self.ops[eng].append(fn)

    def wait(self, eng, sem, v):
        if v > 0:
            self.ops[eng].append(lambda e, s=sem.h, v=v: e.wait_ge(s, v))

    def dma(self, eng, dst, src, sem):
        v = sem.inc(16)
        self.ops[eng].append(lambda e, d=dst, s=src, h=sem.h: e.dma_start(d, s).then_inc(h, 16))
        return v

    def act(self, out, in_, func, sem=None, extra_sem=None, **kw):
        v = sem.inc() if sem is not None else None
        if extra_sem is not None:
            extra_sem.inc()

        def f(e, out=out, in_=in_, func=func, kw=kw, sem=sem, extra_sem=extra_sem):
            e.activation(out, in_, func, **kw)
            d = e.drain()
            if sem is not None:
                d.then_inc(sem.h, 1)
            if extra_sem is not None:
                e.drain().then_inc(extra_sem.h, 1)
        self.ops["scalar"].append(f)
        return v

    def veng(self, eng, fn, sem=None, k=1):
        v = sem.inc(k) if sem is not None else None

        def f(e, fn=fn, sem=sem, k=k):
            ins = fn(e)
            if sem is not None:
                ins.then_inc(sem.h, k)
        self.ops[eng].append(f)
        return v

    def vec(self, fn, sem=None):
        return self.veng("vector", fn, sem)

    def vedge(self, eng, sem):
        self.wait(eng, sem, sem.n)

    def pool(self, fn, sem=None):
        return self.veng("gpsimd", fn, sem)

    def pe(self, fn, sem=None):
        return self.veng("tensor", fn, sem)

    def finish(self):
        with self.nc.Block() as block:
            for eng in self.ENGINES:
                fns = self.ops[eng]
                if fns:
                    getattr(block, eng)(lambda e, fns=fns: [f(e) for f in fns])
        self._es.close()
        return self.nc


def Tc(a):
    return np.ascontiguousarray(np.asarray(a).T)


def pmaj(vec, width=None):
    v = np.asarray(vec)
    if v.ndim == 1:
        v = v[:, None]
    blocks = v.shape[0] // 128
    w = v.shape[1]
    return np.ascontiguousarray(v.reshape(blocks, 128, w).transpose(1, 0, 2).reshape(128, blocks * w))


def fm_ap(dram_ap, ncols_total, col0, nchunks, tcols, row0=0):
    return bass.AP(dram_ap.tensor, row0 * ncols_total + col0, [[ncols_total, 128], [ncols_total * 128, nchunks], [1, tcols]])


def r3(sb_ap, nchunks, tcols):
    return sb_ap.rearrange("p (k t) -> p k t", k=nchunks)


def build_pre(F=None):
    fused = F is not None
    if fused:
        ctx = F["ctx"]
        ctx.push_scope()
        _ph = "pre"
        F["ext"].setdefault(_ph, {})

        def din(name, shape, dtype):
            if name in F["in"].get(_ph, {}):
                return F["in"][_ph][name]
            nm = _ph + "_" + name
            F["ext"][_ph][name] = nm
            return ctx.nc.dram_tensor(nm, shape, dtype, kind="ExternalInput").ap()

        def dout(name, shape, dtype):
            return F["out"][_ph][name]
    else:
        ctx = Ctx()
        din = ctx.din
        dout = ctx.dout
    x0T = din("x0T", [D, L], F32)
    ipT = din("ipT", [D, 2 * DHALF], F32)
    convw = din("convw", [128, 32], F32)
    convb = din("convb", [128, 8], F32)
    xppT = din("xppT", [DHALF, 128], F32)
    ones1f = din("ones1f", [1, 128], F32)
    ones128b = din("ones128b", [128, 1], BF16)
    cvec = din("cvec", [128, 16], F32)

    o_xn0 = dout("xn0", [D, L], F32)
    o_xn16 = dout("xn16", [D, L], BF16)
    o_u16 = dout("u16", [DHALF, L], BF16)
    o_res16 = dout("res16", [DHALF, L], BF16)
    o_xdblp = dout("xdblp", [128, L], F32)

    sx0 = [ctx.sbuf(f"sx0_{i}", [128, KC * TT], F32) for i in range(2)]
    sxb = [ctx.sbuf(f"sxb_{i}", [128, KC * TT], BF16) for i in range(2)]
    ssq1 = ctx.sbuf("ssq1", [128, KC * TT], BF16)
    ssq = [ssq1, ssq1]
    sxn1 = ctx.sbuf("sxn1", [128, KC * TT], F32)
    sxn = [sxn1, sxn1]
    sxnb = [ctx.sbuf(f"sxnb_{i}", [128, KC * TT], BF16) for i in range(2)]
    s_ipT = ctx.sbuf("s_ipT", [128, KC * 2 * DHALF], BF16)
    s_xpp = ctx.sbuf("s_xpp", [128, KC * 128], BF16)
    s_convw = ctx.sbuf("s_convw", [128, 32], F32)
    s_convb = ctx.sbuf("s_convb", [128, 8], F32)
    s_ones1f = ctx.sbuf("s_ones1f", [1, 128], F32)
    s_ones128b = ctx.sbuf("s_ones128b", [128, 1], BF16)
    s_cvec = ctx.sbuf("s_cvec", [128, 16], F32)
    s_m = ctx.sbuf("s_m", [1, TT], F32)
    s_msq = ctx.sbuf("s_msq", [1, TT], F32)
    s_var = ctx.sbuf("s_var", [1, TT], F32)
    s_sd = ctx.sbuf("s_sd", [1, TT], F32)
    s_r = ctx.sbuf("s_r", [1, TT], F32)
    s_conv = [ctx.sbuf(f"s_conv_{k}", [128, TT + 3], F32) for k in range(8)]
    s_cout = [ctx.sbuf(f"s_cout_{i}", [128, TT], F32) for i in range(2)]
    s_u16 = [ctx.sbuf(f"s_u16t_{i}", [128, 8 * TT], BF16) for i in range(2)]
    s_res16 = [ctx.sbuf(f"s_res16t_{i}", [128, 8 * TT], BF16) for i in range(2)]
    s_xdbl = [ctx.sbuf(f"s_xdblt_{i}", [128, TT], F32) for i in range(2)]

    ps_sum = ctx.psum("ps_sum", [1, TT])
    ps_ssq = ctx.psum("ps_ssq", [1, TT])
    ps_mrep = ctx.psum("ps_mrep", [128, TT])
    ps_rrep = ctx.psum("ps_rrep", [128, TT])
    ps_ip = [ctx.psum(f"ps_ip{i}", [128, TT]) for i in range(2)]
    ps_xd = ctx.psum("ps_xd", [128, TT])

    ldc = ctx.sem("ldc")
    wld = ctx.sem("wld")
    ldx = [ctx.sem("ldx0"), ctx.sem("ldx1")]
    vc = ctx.sem("vc")
    sc = ctx.sem("sc")
    tc = ctx.sem("tc")
    ecb = [ctx.sem("ec0"), ctx.sem("ec1")]
    st_xn = ctx.sem("st_xn")
    st_xnb = [ctx.sem("st_xnb0"), ctx.sem("st_xnb1")]
    st_ur = [ctx.sem("st_ur0"), ctx.sem("st_ur1")]
    st_xd = [ctx.sem("st_xd0"), ctx.sem("st_xd1")]

    ctx.dma("sync", s_convw[:, :], convw, ldc)
    ctx.dma("sync", s_convb[:, :], convb, ldc)
    ctx.dma("sync", s_ones1f[:, :], ones1f, ldc)
    ctx.dma("sync", s_ones128b[:, :], ones128b, ldc)
    ctx.dma("sync", s_cvec[:, :], cvec, ldc)
    ctx.dma("gpsimd", r3(s_ipT[:, :], KC, 2 * DHALF), fm_ap(ipT, 2 * DHALF, 0, KC, 2 * DHALF), wld)
    ctx.dma("gpsimd", r3(s_xpp[:, :], KC, 128), fm_ap(xppT, 128, 0, KC, 128), wld)
    v_consts = ldc.n
    v_weights = wld.n

    def zero_tails(e):
        for k in range(7):
            e.memset(s_conv[k][:, 0:3], 0.0)
        return e.memset(s_conv[7][:, 0:3], 0.0)
    ctx.vec(zero_tails, vc)

    ctx.wait("tensor", ldc, v_consts)
    ctx.wait("tensor", wld, v_weights)
    ctx.wait("vector", ldc, v_consts)
    ctx.wait("scalar", ldc, v_consts)

    Mc = {k: [0] * (NT_FULL + 2) for k in
          ("cast", "stats", "var", "rep", "xn", "iplast", "acts", "xd", "xde",
           "ust", "xnst", "xnbst", "xdst")}

    for it in range(NT_FULL):
        bi = it % 2

        if it >= 2:
            ctx.wait("sync", vc, Mc["xn"][it - 2])
        v_ld = ctx.dma("sync", r3(sx0[bi][:, :], KC, TT), fm_ap(x0T, L, it * TT, KC, TT), ldx[bi])

        if PRE_STOP < 1:
            continue
        ctx.wait("vector", ldx[bi], v_ld)
        if it >= 2:
            ctx.wait("vector", tc, Mc["stats"][it - 2])
        if it >= 1:
            ctx.wait("vector", tc, Mc["stats"][it - 1])
        for k in range(KC):
            ctx.vec(lambda e, k=k, bi=bi: e.tensor_copy(sxb[bi][:, k * TT:(k + 1) * TT], sx0[bi][:, k * TT:(k + 1) * TT]))
            last = (k == KC - 1)
            Mc["cast"][it] = ctx.vec(lambda e, k=k, bi=bi: e.tensor_tensor(ssq[bi][:, k * TT:(k + 1) * TT], sx0[bi][:, k * TT:(k + 1) * TT], sx0[bi][:, k * TT:(k + 1) * TT], ALU.mult), vc if last else None) or Mc["cast"][it]

        if PRE_STOP < 2:
            continue
        ctx.wait("tensor", vc, Mc["cast"][it])
        if it >= 1:
            ctx.wait("tensor", vc, Mc["var"][it - 1])

        def mm_stats(e, bi=bi):
            for k in range(KC):
                e.matmul(ps_sum[:, :], s_ones128b[:, :], sxb[bi][:, k * TT:(k + 1) * TT], start=(k == 0), stop=(k == KC - 1))
            for k in range(KC):
                ins = e.matmul(ps_ssq[:, :], s_ones128b[:, :], ssq[bi][:, k * TT:(k + 1) * TT], start=(k == 0), stop=(k == KC - 1))
            return ins
        Mc["stats"][it] = ctx.pe(mm_stats, tc)

        if PRE_STOP < 3:
            continue
        ctx.wait("vector", tc, Mc["stats"][it])
        if it >= 1:
            ctx.wait("vector", tc, Mc["rep"][it - 1])
        ctx.vec(lambda e: e.tensor_scalar(s_m[:, :], ps_sum[:, :], 1.0 / D, None, ALU.mult), vc)
        ctx.vedge("vector", vc)
        ctx.vec(lambda e: e.tensor_tensor(s_msq[:, :], s_m[:, :], s_m[:, :], ALU.mult), vc)
        ctx.vedge("vector", vc)
        Mc["var"][it] = ctx.vec(lambda e: e.scalar_tensor_tensor(s_var[:, :], ps_ssq[:, :], 1.0 / D, s_msq[:, :], ALU.mult, ALU.subtract), vc)
        ctx.wait("scalar", vc, Mc["var"][it])
        v_sd = ctx.act(s_sd[:, :], s_var[:, :], AF.Sqrt, sem=sc, bias=s_cvec[0:1, 0:1])
        ctx.wait("vector", sc, v_sd)
        v_r = ctx.vec(lambda e: e.reciprocal(s_r[:, :], s_sd[:, :]), vc)

        if PRE_STOP < 4:
            continue
        ctx.wait("tensor", vc, v_r)
        if it >= 1:
            ctx.wait("tensor", vc, Mc["xn"][it - 1])

        def mm_rep(e):
            e.matmul(ps_mrep[:, :], s_ones1f[:, :], s_m[:, :], start=True, stop=True)
            return e.matmul(ps_rrep[:, :], s_ones1f[:, :], s_r[:, :], start=True, stop=True)
        Mc["rep"][it] = ctx.pe(mm_rep, tc)

        if PRE_STOP < 5:
            continue
        ctx.wait("vector", tc, Mc["rep"][it])
        if it >= 2:
            ctx.wait("vector", tc, Mc["iplast"][it - 2])
            ctx.wait("vector", st_xnb[bi], Mc["xnbst"][it - 2])
        if it >= 1:
            ctx.wait("vector", st_xn, Mc["xnst"][it - 1])
        for k in range(KC):
            ctx.vec(lambda e, k=k, bi=bi: e.tensor_tensor(sxn[bi][:, k * TT:(k + 1) * TT], sx0[bi][:, k * TT:(k + 1) * TT], ps_mrep[:, :], ALU.subtract), vc if k == KC - 1 else None)
        ctx.vedge("vector", vc)
        for k in range(KC):
            ctx.vec(lambda e, k=k, bi=bi: e.tensor_tensor(sxn[bi][:, k * TT:(k + 1) * TT], sxn[bi][:, k * TT:(k + 1) * TT], ps_rrep[:, :], ALU.mult), vc if k == KC - 1 else None)
        ctx.vedge("vector", vc)
        for k in range(KC):
            last = (k == KC - 1)
            Mc["xn"][it] = ctx.vec(lambda e, k=k, bi=bi: e.tensor_copy(sxnb[bi][:, k * TT:(k + 1) * TT], sxn[bi][:, k * TT:(k + 1) * TT]), vc if last else None) or Mc["xn"][it]

        if PRE_STOP < 6:
            continue
        ctx.wait("sync", vc, Mc["xn"][it])
        Mc["xnst"][it] = ctx.dma("sync", fm_ap(o_xn0, L, it * TT, KC, TT), r3(sxn[bi][:, :], KC, TT), st_xn)
        Mc["xnbst"][it] = ctx.dma("sync", fm_ap(o_xn16, L, it * TT, KC, TT), r3(sxnb[bi][:, :], KC, TT), st_xnb[bi])

        if PRE_STOP < 7:
            continue
        ctx.wait("tensor", vc, Mc["xn"][it])
        if it == 0:
            ctx.wait("vector", vc, 1)
        if it >= 1:
            ctx.wait("vector", sc, Mc["acts"][it - 1])
        silu_v = [0] * 8
        for oc in range(16):
            pb = oc % 2
            gidx = it * 16 + oc
            if gidx >= 2:
                ctx.wait("tensor", ecb[pb], gidx // 2)

            def mm_ip(e, oc=oc, pb=pb, bi=bi):
                for k in range(KC):
                    ins = e.matmul(ps_ip[pb][:, :], s_ipT[:, k * 2 * DHALF + oc * 128: k * 2 * DHALF + (oc + 1) * 128],
                             sxnb[bi][:, k * TT:(k + 1) * TT], start=(k == 0), stop=(k == KC - 1))
                return ins
            v_mm = ctx.pe(mm_ip, tc)
            if oc == 15:
                Mc["iplast"][it] = v_mm

            if oc < 8:
                blk = oc
                ctx.wait("vector", tc, v_mm)
                if it > 0:
                    ctx.vec(lambda e, blk=blk: e.tensor_copy(s_conv[blk][:, 0:3], s_conv[blk][:, TT:TT + 3]), vc)
                    ctx.vedge("vector", vc)
                ctx.vec(lambda e, blk=blk, pb=pb: e.tensor_copy(s_conv[blk][:, 3:TT + 3], ps_ip[pb][:, :]), ecb[pb])
                ctx.vedge("vector", ecb[pb])
                cb = blk % 2
                if blk >= 2:
                    ctx.wait("vector", sc, silu_v[blk - 2])
                ctx.vec(lambda e, blk=blk, cb=cb: e.tensor_scalar(s_cout[cb][:, :], s_conv[blk][:, 0:TT], s_convw[:, 4 * blk:4 * blk + 1], None, ALU.mult), vc)
                for kk in range(1, 4):
                    ctx.vedge("vector", vc)
                    ctx.vec(lambda e, blk=blk, kk=kk, cb=cb: e.scalar_tensor_tensor(s_cout[cb][:, :], s_conv[blk][:, kk:TT + kk], s_convw[:, 4 * blk + kk:4 * blk + kk + 1], s_cout[cb][:, :], ALU.mult, ALU.add), vc)
                v_conv = vc.n
                ctx.wait("scalar", vc, v_conv)
                if it >= 2 and blk == 0:
                    ctx.wait("scalar", st_ur[bi], Mc["ust"][it - 2])
                    ctx.wait("scalar", tc, Mc["xd"][it - 2])
                silu_v[blk] = ctx.act(s_u16[bi][:, blk * TT:(blk + 1) * TT], s_cout[cb][:, :], AF.Silu, sem=sc, bias=s_convb[:, blk:blk + 1])
            else:
                blk = oc - 8
                ctx.wait("scalar", tc, v_mm)
                ctx.act(s_res16[bi][:, blk * TT:(blk + 1) * TT], ps_ip[pb][:, :], AF.Silu, sem=sc, extra_sem=ecb[pb])
        Mc["acts"][it] = sc.n

        if PRE_STOP < 8:
            continue
        ctx.wait("sync", sc, Mc["acts"][it])
        ctx.dma("sync", bass.AP(o_u16.tensor, it * TT, [[L, 128], [L * 128, 8], [1, TT]]), r3(s_u16[bi][:, :], 8, TT), st_ur[bi])
        Mc["ust"][it] = ctx.dma("sync", bass.AP(o_res16.tensor, it * TT, [[L, 128], [L * 128, 8], [1, TT]]), r3(s_res16[bi][:, :], 8, TT), st_ur[bi])

        if PRE_STOP < 9:
            continue
        ctx.wait("tensor", sc, Mc["acts"][it])
        if it >= 1:
            ctx.wait("tensor", vc, Mc["xde"][it - 1])

        def mm_xd(e, bi=bi):
            for k in range(KC):
                ins = e.matmul(ps_xd[:, :], s_xpp[:, k * 128:(k + 1) * 128], s_u16[bi][:, k * TT:(k + 1) * TT], start=(k == 0), stop=(k == KC - 1))
            return ins
        Mc["xd"][it] = ctx.pe(mm_xd, tc)
        ctx.wait("vector", tc, Mc["xd"][it])
        if it >= 2:
            ctx.wait("vector", st_xd[bi], Mc["xdst"][it - 2])
        Mc["xde"][it] = ctx.vec(lambda e, bi=bi: e.tensor_copy(s_xdbl[bi][:, :], ps_xd[:, :]), vc)
        ctx.wait("sync", vc, Mc["xde"][it])
        Mc["xdst"][it] = ctx.dma("sync", o_xdblp[:, it * TT:(it + 1) * TT], s_xdbl[bi][:, :], st_xd[bi])

    for s in (st_xn, st_xnb[0], st_xnb[1], st_ur[0], st_ur[1], st_xd[0], st_xd[1]):
        ctx.wait("sync", s, s.n)
    if fused:
        ctx.pop_scope()
    return ctx


NCH = 8
SC_STOP = 99
SC_NBLK = 8
SC_SKIP_ACT = False
SC_SKIP_TS = False
SC_ACT_N = 3
SC_ABS_NOBIAS = False
SC_DL_COPY = False
TC = 512


def build_scan(F=None):
    fused = F is not None
    if fused:
        ctx = F["ctx"]
        ctx.push_scope()
        _ph = "scan"
        F["ext"].setdefault(_ph, {})

        def din(name, shape, dtype):
            if name in F["in"].get(_ph, {}):
                return F["in"][_ph][name]
            nm = _ph + "_" + name
            F["ext"][_ph][name] = nm
            return ctx.nc.dram_tensor(nm, shape, dtype, kind="ExternalInput").ap()

        def dout(name, shape, dtype):
            return F["out"][_ph][name]
    else:
        ctx = Ctx()
        din = ctx.din
        dout = ctx.dout
    xdbl0 = din("xdbl0", [128, L], F32)
    xdbl1 = None if fused else din("xdbl1", [128, L], F32)
    u16 = din("u16i", [DHALF, L], BF16)
    res16 = din("res16i", [DHALF, L], BF16)
    dtpT = din("dtpT", [RK, DHALF], F32)
    dtb = din("dtb", [128, 8], F32)
    AlogP = din("AlogP", [128, 128], F32)
    DskP = din("DskP", [128, 8], F32)
    opT = din("opT", [DHALF, DHALF], F32)
    sel16 = din("sel16", [48, 16 * 128], BF16)
    ident = din("ident", [128, 128], BF16)

    o_mp = dout("mp", [2048, 2048], F32) if fused else dout("mp", [DHALF, L], F32)
    o_ym = dout("ym", [DHALF, L], BF16)

    s_xd = ctx.sbuf("s_xd", [128, L], F32)
    s_xd1 = None if fused else ctx.sbuf("s_xd1", [128, L], F32)
    s_bc16 = ctx.sbuf("s_bc16", [48, L], BF16)
    s_dt16 = ctx.sbuf("s_dt16", [64, L], BF16)
    s_dtpT = ctx.sbuf("s_dtpT", [RK, DHALF], BF16)
    s_dtb = ctx.sbuf("s_dtb", [128, 8], F32)
    s_Alog = ctx.sbuf("s_Alog", [128, 128], F32)
    s_A = ctx.sbuf("s_A", [128, 128], F32)
    s_Dsk = ctx.sbuf("s_Dsk", [128, 8], F32)
    s_opT = ctx.sbuf("s_opT", [128, KC * DHALF], BF16)
    s_sel = ctx.sbuf("s_sel", [48, 16 * 128], BF16)
    s_id = ctx.sbuf("s_id", [128, 128], BF16)
    s_uC = [ctx.sbuf(f"s_uC{i}", [128, 8 * TC], BF16) for i in range(2)]
    s_rC = [ctx.sbuf(f"s_rC{i}", [128, 8 * TC], BF16) for i in range(2)]
    s_dC = [ctx.sbuf(f"s_dC{i}", [128, 8 * TC], F32) for i in range(2)]
    s_sp1 = ctx.sbuf("s_sp1", [128, TC], F32)
    s_sp2 = ctx.sbuf("s_sp2", [128, TC], F32)
    s_sp3 = ctx.sbuf("s_sp3", [128, TC], F32)
    s_du = [ctx.sbuf(f"s_du{i}", [128, TC], F32) for i in range(2)]
    s_dA = [ctx.sbuf(f"s_dA{i}", [128, TC], F32) for i in range(2)]
    s_o = [ctx.sbuf(f"s_o{i}", [128, TC], F32) for i in range(2)]
    s_h = [ctx.sbuf(f"s_h{i}", [128, TC], BF16) for i in range(2)]
    s_hc = [ctx.sbuf(f"s_hc{i}", [128, TC], BF16) for i in range(2)]
    s_brepC = ctx.sbuf("s_brepC", [128, 16 * TC], BF16)
    s_crepC = ctx.sbuf("s_crepC", [128, 16 * TC], BF16)
    s_hst = ctx.sbuf("s_hst", [128, 128], F32)
    s_t1 = ctx.sbuf("s_t1", [128, TC], F32)
    s_t2 = ctx.sbuf("s_t2", [128, TC], F32)
    s_ymC = [ctx.sbuf(f"s_ymC{i}", [128, 8 * TC], BF16) for i in range(2)]
    s_mp = [ctx.sbuf(f"s_mp{i}", [128, TC], F32) for i in range(2)]

    ps_d = [ctx.psum(f"ps_d{i}", [128, TC]) for i in range(2)]
    ps_y = [ctx.psum(f"ps_y{i}", [128, TC]) for i in range(2)]
    ps_br = ctx.psum("ps_br", [128, TC])
    ps_cr = ctx.psum("ps_cr", [128, TC])
    ps_mp = [ctx.psum(f"ps_mp{i}", [128, TC]) for i in range(2)]

    ldc = ctx.sem("ldc")
    wld = ctx.sem("wld")
    ldu = [ctx.sem("ldu0"), ctx.sem("ldu1")]
    ldr = [ctx.sem("ldr0"), ctx.sem("ldr1")]
    vc = ctx.sem("vc")
    sc = ctx.sem("sc")
    tc = ctx.sem("tc")
    pc = ctx.sem("pc")
    st_mp = [ctx.sem("st_mp0"), ctx.sem("st_mp1")]
    st_ym = [ctx.sem("st_ym0"), ctx.sem("st_ym1")]

    ctx.dma("sync", s_xd[:, :], xdbl0, ldc)
    if not fused:
        ctx.dma("sync", s_xd1[:, :], xdbl1, ldc)
    ctx.dma("sync", s_dtb[:, :], dtb, ldc)
    ctx.dma("sync", s_Alog[:, :], AlogP, ldc)
    ctx.dma("sync", s_Dsk[:, :], DskP, ldc)
    ctx.dma("sync", s_sel[:, :], sel16, ldc)
    ctx.dma("sync", s_id[:, :], ident, ldc)
    ctx.dma("gpsimd", s_dtpT[:, :], dtpT, wld)
    ctx.dma("gpsimd", r3(s_opT[:, :], KC, DHALF), fm_ap(opT, DHALF, 0, KC, DHALF), wld)
    v_ldc = ldc.n
    v_wld = wld.n

    ctx.wait("vector", ldc, v_ldc)
    if not fused:
        for j in range(8):
            ctx.vec(lambda e, j=j: e.tensor_tensor(s_xd[:, j * TC:(j + 1) * TC], s_xd[:, j * TC:(j + 1) * TC], s_xd1[:, j * TC:(j + 1) * TC], ALU.add),
                    vc if j == 7 else None)
        ctx.vedge("vector", vc)
    for j in range(4):
        ctx.vec(lambda e, j=j: e.tensor_copy(s_bc16[:, j * 1024:(j + 1) * 1024], s_xd[0:48, j * 1024:(j + 1) * 1024]),
                vc if j == 3 else None)
    v_bc = vc.n
    for j in range(4):
        ctx.vec(lambda e, j=j: e.tensor_copy(s_dt16[:, j * 1024:(j + 1) * 1024], s_xd[64:128, j * 1024:(j + 1) * 1024]),
                vc if j == 3 else None)
    v_dt = vc.n
    ctx.wait("scalar", ldc, v_ldc)
    v_exp = ctx.act(s_A[:, :], s_Alog[:, :], AF.Exp, sem=sc)
    ctx.wait("vector", sc, v_exp)
    v_A = ctx.vec(lambda e: e.tensor_scalar(s_A[:, :], s_A[:, :], -1.0, None, ALU.mult), vc)
    v_h0 = ctx.vec(lambda e: e.memset(s_hst[:, :], 0.0), vc)

    ctx.wait("tensor", wld, v_wld)
    ctx.wait("tensor", ldc, v_ldc)
    ctx.wait("tensor", vc, v_dt)

    Msc = {k: [0] * (NCH + 2) for k in
           ("delta", "ymdone", "ymst", "mpdone", "ub", "rb", "scn")}
    kst_mp = [[0] * (NCH * KC + 4) for _ in range(2)]
    kdf = [0, 0]
    kdl_last = 0
    ka3_last = 0
    ktts = [0, 0]
    khc = [0, 0]
    kpo = [0, 0]
    kmy = [0, 0]
    kye = [0, 0]
    kmpe = [0, 0]
    kye_last = 0

    for c in range(NCH):
        bi = c % 2

        if c >= 2:
            ctx.wait("sync", vc, Msc["ymdone"][c - 2])
        v_lu = ctx.dma("sync", r3(s_uC[bi][:, :], 8, TC), bass.AP(u16.tensor, c * TC, [[L, 128], [L * 128, 8], [1, TC]]), ldu[bi])
        v_lr = ctx.dma("sync", r3(s_rC[bi][:, :], 8, TC), bass.AP(res16.tensor, c * TC, [[L, 128], [L * 128, 8], [1, TC]]), ldr[bi])
        Msc["ub"][c] = v_lu
        Msc["rb"][c] = v_lr

        if SC_STOP < 2:
            continue
        for blk in range(SC_NBLK):
            pd = blk % 2
            gix = c * 8 + blk
            if gix >= 2:
                ctx.wait("tensor", vc, kdf[pd])

            def mm_delta(e, blk=blk, pd=pd, c=c):
                return e.matmul(ps_d[pd][:, :], s_dtpT[:, blk * 128:(blk + 1) * 128], s_dt16[:, c * TC:(c + 1) * TC], start=True, stop=True)
            v_md = ctx.pe(mm_delta, tc)

            ctx.wait("vector", tc, v_md)
            ctx.wait("vector", vc, kdl_last)
            if SC_SKIP_TS:
                v_r1 = ctx.vec(lambda e, pd=pd: e.tensor_copy(s_sp1[:, :], ps_d[pd][:, :]), vc)
            else:
                v_r1 = ctx.vec(lambda e, pd=pd, blk=blk: e.tensor_scalar(s_sp1[:, :], ps_d[pd][:, :], s_dtb[:, blk:blk + 1], 0.0, ALU.add, ALU.max), vc)
            if SC_SKIP_ACT:
                ctx.wait("vector", vc, v_r1)
                if c >= 1 and blk == 0:
                    ctx.wait("vector", vc, Msc["scn"][c - 1])
                v_dl = ctx.vec(lambda e, bi=bi, blk=blk: e.tensor_copy(s_dC[bi][:, blk * TC:(blk + 1) * TC], s_sp1[:, :]), vc)
            else:
                ctx.wait("scalar", tc, v_md)
                ctx.wait("scalar", vc, v_r1)
                ctx.wait("scalar", sc, ka3_last)
                ctx.wait("scalar", vc, kdl_last)
                if SC_ABS_NOBIAS:
                    v_a1 = ctx.act(s_sp2[:, :], ps_d[pd][:, :], AF.Abs, sem=sc)
                else:
                    v_a1 = ctx.act(s_sp2[:, :], ps_d[pd][:, :], AF.Abs, sem=sc, bias=s_dtb[:, blk:blk + 1])
                ka3_last = v_a1
                if SC_ACT_N >= 2:
                    ctx.wait("scalar", sc, v_a1)
                    v_a2 = ctx.act(s_sp3[:, :], s_sp2[:, :], AF.Exp, sem=sc, scale=-1.0)
                    ka3_last = v_a2
                if SC_ACT_N >= 3:
                    ctx.wait("scalar", sc, v_a2)
                    v_a3 = ctx.act(s_sp2[:, :], s_sp3[:, :], AF.Ln, sem=sc, bias=1.0)
                    ka3_last = v_a3
                ctx.wait("vector", sc, ka3_last)
                ctx.wait("vector", vc, v_r1)
                if c >= 1 and blk == 0:
                    ctx.wait("vector", vc, Msc["scn"][c - 1])
                if SC_DL_COPY:
                    v_dl = ctx.vec(lambda e, bi=bi, blk=blk: e.tensor_copy(s_dC[bi][:, blk * TC:(blk + 1) * TC], s_sp1[:, :]), vc)
                else:
                    v_dl = ctx.vec(lambda e, bi=bi, blk=blk: e.tensor_tensor(s_dC[bi][:, blk * TC:(blk + 1) * TC], s_sp1[:, :], s_sp2[:, :], ALU.add), vc)
            kdf[pd] = v_dl
            kdl_last = v_dl
        Msc["delta"][c] = vc.n

        if SC_STOP < 3:
            continue
        ctx.wait("vector", ldu[bi], Msc["ub"][c])
        kbe_n = [0] * 16
        for n in range(16):
            if c >= 1 and n == 0:
                ctx.wait("vector", vc, Msc["scn"][c - 1])
            if n >= 1:
                ctx.wait("tensor", vc, kbe_n[n - 1])

            def mm_br(e, n=n, c=c):
                e.matmul(ps_br[:, :], s_sel[0:16, n * 128:(n + 1) * 128], s_bc16[0:16, c * TC:(c + 1) * TC], start=True, stop=True)
                return e.matmul(ps_cr[:, :], s_sel[32:48, n * 128:(n + 1) * 128], s_bc16[32:48, c * TC:(c + 1) * TC], start=True, stop=True)
            v_br = ctx.pe(mm_br, tc)
            ctx.wait("vector", tc, v_br)
            ctx.vec(lambda e, n=n: e.tensor_copy(s_brepC[:, n * TC:(n + 1) * TC], ps_br[:, :]))
            kbe_n[n] = ctx.vec(lambda e, n=n: e.tensor_copy(s_crepC[:, n * TC:(n + 1) * TC], ps_cr[:, :]), vc)
        v_bcache = kbe_n[15]
        ctx.wait("gpsimd", vc, v_bcache)

        for blk in range(8):
            ctx.wait("vector", vc, Msc["delta"][c])
            db = blk % 2
            if blk >= 2:
                ctx.wait("vector", pc, kpo[db])
            v_du = ctx.vec(lambda e, bi=bi, blk=blk, db=db: e.tensor_tensor(s_du[db][:, :], s_dC[bi][:, blk * TC:(blk + 1) * TC], s_uC[bi][:, blk * TC:(blk + 1) * TC], ALU.mult), vc)
            for n in range(16):
                nb = n % 2
                gin = (c * 8 + blk) * 16 + n
                col = blk * 16 + n
                if gin >= 2:
                    ctx.wait("scalar", vc, ktts[nb])
                if gin == 0:
                    ctx.wait("scalar", vc, v_A)
                ctx.wait("scalar", vc, Msc["delta"][c])
                v_dA = ctx.act(s_dA[nb][:, :], s_dC[bi][:, blk * TC:(blk + 1) * TC], AF.Exp, sem=sc, scale=s_A[:, col:col + 1])

                ctx.wait("gpsimd", vc, v_du)
                if gin >= 2:
                    ctx.wait("gpsimd", vc, ktts[nb])
                v_o = ctx.pool(lambda e, nb=nb, db=db, n=n: e.tensor_tensor(s_o[nb][:, :], s_du[db][:, :], s_brepC[:, n * TC:(n + 1) * TC], ALU.mult), pc)
                kpo[db] = v_o

                ctx.wait("vector", sc, v_dA)
                ctx.wait("vector", pc, v_o)
                if gin >= 2:
                    ctx.wait("vector", vc, khc[nb])
                v_tts = ctx.vec(lambda e, nb=nb, col=col: e.tensor_tensor_scan(s_h[nb][:, :], s_dA[nb][:, :], s_o[nb][:, :], s_hst[:, col:col + 1], ALU.mult, ALU.add), vc)
                ktts[nb] = v_tts

                ctx.vedge("vector", vc)
                ctx.vec(lambda e, nb=nb, col=col: e.tensor_copy(s_hst[:, col:col + 1], s_h[nb][:, TC - 1:TC]))
                if gin >= 2:
                    ctx.wait("vector", tc, kmy[nb])
                v_hc = ctx.vec(lambda e, nb=nb, n=n: e.tensor_tensor(s_hc[nb][:, :], s_h[nb][:, :], s_crepC[:, n * TC:(n + 1) * TC], ALU.mult), vc)
                khc[nb] = v_hc

                yb = blk % 2
                ctx.wait("tensor", vc, v_hc)
                if n == 0 and (c * 8 + blk) >= 2:
                    ctx.wait("tensor", vc, kye[yb])

                def mm_y(e, nb=nb, yb=yb, n=n):
                    return e.matmul(ps_y[yb][:, :], s_id[:, :], s_hc[nb][:, :], start=(n == 0), stop=(n == 15))
                v_my = ctx.pe(mm_y, tc)
                kmy[nb] = v_my

            ctx.wait("vector", tc, v_my)
            ctx.wait("vector", ldr[bi], Msc["rb"][c])
            ctx.wait("vector", vc, kye_last)
            v_t1 = ctx.vec(lambda e, bi=bi, blk=blk: e.tensor_scalar(s_t1[:, :], s_uC[bi][:, blk * TC:(blk + 1) * TC], s_Dsk[:, blk:blk + 1], None, ALU.mult), vc)
            ctx.vedge("vector", vc)
            v_t2 = ctx.vec(lambda e, yb=yb: e.tensor_tensor(s_t2[:, :], ps_y[yb][:, :], s_t1[:, :], ALU.add), vc)
            kye[yb] = v_t2
            kye_last = v_t2
            ctx.vedge("vector", vc)
            if c >= 2 and blk == 0:
                ctx.wait("vector", st_ym[bi], Msc["ymst"][c - 2])
                ctx.wait("vector", tc, Msc["mpdone"][c - 2])
            v_ym = ctx.vec(lambda e, bi=bi, blk=blk: e.tensor_tensor(s_ymC[bi][:, blk * TC:(blk + 1) * TC], s_t2[:, :], s_rC[bi][:, blk * TC:(blk + 1) * TC], ALU.mult), vc)
        Msc["ymdone"][c] = vc.n
        Msc["scn"][c] = vc.n

        if SC_STOP < 4:
            continue
        ctx.wait("sync", vc, Msc["ymdone"][c])
        Msc["ymst"][c] = ctx.dma("sync", bass.AP(o_ym.tensor, c * TC, [[L, 128], [L * 128, 8], [1, TC]]), r3(s_ymC[bi][:, :], 8, TC), st_ym[bi])

        ctx.wait("tensor", vc, Msc["ymdone"][c])
        for ocb in range(KC):
            pm = ocb % 2
            gmo = c * KC + ocb
            if gmo >= 2:
                ctx.wait("tensor", vc, kmpe[pm])

            def mm_mp(e, ocb=ocb, pm=pm, bi=bi):
                for kblk in range(KC):
                    ins = e.matmul(ps_mp[pm][:, :], s_opT[:, kblk * DHALF + ocb * 128: kblk * DHALF + (ocb + 1) * 128],
                                   s_ymC[bi][:, kblk * TC:(kblk + 1) * TC], start=(kblk == 0), stop=(kblk == KC - 1))
                return ins
            v_mp = ctx.pe(mm_mp, tc)
            ctx.wait("vector", tc, v_mp)
            mb = ocb % 2
            if gmo >= 2:
                ctx.wait("vector", st_mp[mb], kst_mp[mb][gmo - 2])
            v_me = ctx.vec(lambda e, mb=mb, pm=pm: e.tensor_copy(s_mp[mb][:, :], ps_mp[pm][:, :]), vc)
            kmpe[pm] = v_me
            ctx.wait("sync", vc, v_me)
            if fused:
                half = c // 4
                mpoff = (half * 1024 + ocb * 128) * 2048 + (c % 4) * TC
                kst_mp[mb][gmo] = ctx.dma("sync", bass.AP(o_mp.tensor, mpoff, [[2048, 128], [1, TC]]), s_mp[mb][:, :], st_mp[mb])
            else:
                kst_mp[mb][gmo] = ctx.dma("sync", bass.AP(o_mp.tensor, ocb * 128 * L + c * TC, [[L, 128], [1, TC]]), s_mp[mb][:, :], st_mp[mb])
        Msc["mpdone"][c] = tc.n

    for s in (st_mp[0], st_mp[1], st_ym[0], st_ym[1]):
        ctx.wait("sync", s, s.n)
    if fused:
        ctx.pop_scope()
    return ctx


def host_prep_scan(inp):
    f32 = np.float32
    P = {}
    P["dtpT"] = [np.ascontiguousarray(Tc(np.asarray(inp["L0_attn_dt_proj_w"], f32))[:, h * DHALF:(h + 1) * DHALF]) for h in range(2)]
    P["dtb"] = [pmaj(np.asarray(inp["L0_attn_dt_proj_b"], f32)[h * DHALF:(h + 1) * DHALF]) for h in range(2)]
    P["AlogP"] = [pmaj(np.asarray(inp["L0_attn_A_log"], f32)[h * DHALF:(h + 1) * DHALF]) for h in range(2)]
    P["DskP"] = [pmaj(np.asarray(inp["L0_attn_D_skip"], f32)[h * DHALF:(h + 1) * DHALF]) for h in range(2)]
    P["opT"] = [np.ascontiguousarray(Tc(np.asarray(inp["L0_attn_out_proj_w"], f32))[h * DHALF:(h + 1) * DHALF]) for h in range(2)]
    sel = np.zeros((48, 16 * 128), NB16)
    for n in range(16):
        sel[n, n * 128:(n + 1) * 128] = 1
        sel[32 + n, n * 128:(n + 1) * 128] = 1
    P["sel16"] = sel
    P["ident"] = np.eye(128, dtype=NB16)
    return P


def maps_scan(P, r1):
    maps = []
    for c in range(8):
        b, h = c // 2, c % 2
        maps.append({
            "xdbl0": r1[2 * b]["xdblp"], "xdbl1": r1[2 * b + 1]["xdblp"],
            "u16i": r1[c]["u16"], "res16i": r1[c]["res16"],
            "dtpT": P["dtpT"][h], "dtb": P["dtb"][h], "AlogP": P["AlogP"][h],
            "DskP": P["DskP"][h], "opT": P["opT"][h], "sel16": P["sel16"],
            "ident": P["ident"],
        })
    return maps


HID = 3072
HC = HID // 128
FFN0_DBG = False
FFN0_DBG2 = False


def build_ffn0(F=None):
    fused = F is not None
    if fused:
        ctx = F["ctx"]
        ctx.push_scope()
        _ph = "ffn0"
        F["ext"].setdefault(_ph, {})

        def din(name, shape, dtype):
            if name in F["in"].get(_ph, {}):
                return F["in"][_ph][name]
            nm = _ph + "_" + name
            F["ext"][_ph][name] = nm
            return ctx.nc.dram_tensor(nm, shape, dtype, kind="ExternalInput").ap()

        def dout(name, shape, dtype):
            return F["out"][_ph][name]
    else:
        ctx = Ctx()
        din = ctx.din
        dout = ctx.dout
    xn16 = din("xn16i", [D, L] if (fused and not FFN0_DBG) else [D, 2048], BF16)
    wgT = din("wgT", [D, HID], I8)
    wuT = din("wuT", [D, HID], I8)
    wdT = din("wdT", [HID, D], I8)
    cvec = din("cvec", [128, 16], F32)
    o_g0 = dout("g0", [D, 2048], BF16)

    s_x = [ctx.sbuf(f"s_x{i}", [128, KC * TT], BF16) for i in range(2)]
    s_wg = [ctx.sbuf(f"s_wg{i}", [128, KC * 128], BF16) for i in range(2)]
    s_wu = [ctx.sbuf(f"s_wu{i}", [128, KC * 128], BF16) for i in range(2)]
    s_wd = ctx.sbuf("s_wd", [128, HC * D], BF16)
    s_sg = ctx.sbuf("s_sg", [128, TT], F32)
    s_h = ctx.sbuf("s_h", [128, HC * TT], BF16)
    s_g0 = [ctx.sbuf(f"s_g0{i}", [128, TT], BF16) for i in range(2)]
    s_cvec = ctx.sbuf("s_cvec", [128, 16], F32)

    ps_g = [ctx.psum(f"ps_g{i}", [128, TT]) for i in range(2)]
    ps_u = [ctx.psum(f"ps_u{i}", [128, TT]) for i in range(2)]
    ps_o = [ctx.psum(f"ps_o{i}", [128, TT]) for i in range(2)]

    ldc = ctx.sem("ldc")
    wld = ctx.sem("wld")
    ldx = [ctx.sem("ldx0"), ctx.sem("ldx1")]
    lwg = [ctx.sem("lwg0"), ctx.sem("lwg1")]
    vc = ctx.sem("vc")
    sc = ctx.sem("sc")
    tc = ctx.sem("tc")
    st_g = [ctx.sem("st_g0"), ctx.sem("st_g1")]

    ctx.dma("sync", s_cvec[:, :], cvec, ldc)
    ctx.dma("gpsimd", r3(s_wd[:, :], HC, D), fm_ap(wdT, D, 0, HC, D), wld)
    v_ldc = ldc.n
    v_wld = wld.n
    ctx.wait("scalar", ldc, v_ldc)
    ctx.wait("tensor", wld, v_wld)

    NTI = 4
    kx = [0, 0]
    kwg = [0, 0]
    khmm = [0, 0, 0, 0]
    ksg = 0
    kmul = [0] * (HC + 2)
    kdown = [0, 0]
    kg0e = [0, 0]
    kg0st = [[0] * (NTI * KC + 4) for _ in range(2)]
    klastdown = 0

    for it in range(NTI):
        bi = it % 2
        if it >= 2:
            ctx.wait("sync", tc, kx[bi])
        if fused and not FFN0_DBG:
            v_lx = F["dyn_dma"](ctx, r3(s_x[bi][:, :], KC, TT), xn16, it * TT, [[L, 128], [L * 128, KC], [1, TT]], ldx[bi])
        else:
            v_lx = ctx.dma("sync", r3(s_x[bi][:, :], KC, TT), fm_ap(xn16, 2048, it * TT, KC, TT), ldx[bi])

        for hc in range(HC):
            wb = hc % 2
            gi = it * HC + hc
            if gi >= 2:
                ctx.wait("gpsimd", tc, khmm[wb])
            v_wg = ctx.dma("gpsimd", r3(s_wg[wb][:, :], KC, 128), fm_ap(wgT, HID, hc * 128, KC, 128), lwg[wb])
            v_wu = ctx.dma("gpsimd", r3(s_wu[wb][:, :], KC, 128), fm_ap(wuT, HID, hc * 128, KC, 128), lwg[wb])

            ctx.wait("tensor", lwg[wb], v_wu)
            ctx.wait("tensor", ldx[bi], v_lx)
            pp = hc % 2
            if gi >= 2:
                ctx.wait("tensor", sc, ksg_par[pp])
                ctx.wait("tensor", vc, kmul[hc - 2])

            def mm_gu(e, wb=wb, pp=pp, bi=bi):
                for k in range(KC):
                    e.matmul(ps_g[pp][:, :], s_wg[wb][:, k * 128:(k + 1) * 128], s_x[bi][:, k * TT:(k + 1) * TT], start=(k == 0), stop=(k == KC - 1))
                for k in range(KC):
                    ins = e.matmul(ps_u[pp][:, :], s_wu[wb][:, k * 128:(k + 1) * 128], s_x[bi][:, k * TT:(k + 1) * TT], start=(k == 0), stop=(k == KC - 1))
                return ins
            v_mm = ctx.pe(mm_gu, tc)
            khmm[wb] = v_mm

            ctx.wait("scalar", tc, v_mm)
            ctx.wait("scalar", vc, ksg)
            v_sg = ctx.act(s_sg[:, :], ps_g[pp][:, :], AF.Silu, sem=sc, scale=s_cvec[:, 2:3])
            if gi == 0:
                ksg_par = [0, 0]
            ksg_par[pp] = v_sg
            ctx.wait("vector", sc, v_sg)
            if it >= 1 and hc == 0:
                ctx.wait("vector", tc, klastdown)
            v_mul = ctx.vec(lambda e, hc=hc, pp=pp: e.tensor_tensor(s_h[:, hc * TT:(hc + 1) * TT], s_sg[:, :], ps_u[pp][:, :], ALU.mult), vc)
            kmul[hc] = v_mul
            ksg = v_mul

        ctx.wait("tensor", vc, kmul[HC - 1])
        for oc in range(KC):
            pb = oc % 2
            go = it * KC + oc
            if go >= 2:
                ctx.wait("tensor", sc, kg0e[pb])

            def mm_d(e, oc=oc, pb=pb):
                for k in range(HC):
                    ins = e.matmul(ps_o[pb][:, :], s_wd[:, k * D + oc * 128: k * D + (oc + 1) * 128], s_h[:, k * TT:(k + 1) * TT], start=(k == 0), stop=(k == HC - 1))
                return ins
            v_md = ctx.pe(mm_d, tc)
            klastdown = v_md

            gb = oc % 2
            ctx.wait("scalar", tc, v_md)
            if go >= 2:
                ctx.wait("scalar", st_g[gb], kg0st[gb][go - 2])
            v_g0 = ctx.act(s_g0[gb][:, :], ps_o[pb][:, :], AF.Silu, sem=sc, scale=s_cvec[:, 3:4])
            kg0e[pb] = v_g0
            ctx.wait("sync", sc, v_g0)
            kg0st[gb][go] = ctx.dma("sync", bass.AP(o_g0.tensor, oc * 128 * 2048 + it * TT, [[2048, 128], [1, TT]]), s_g0[gb][:, :], st_g[gb])
        kx[bi] = klastdown if False else tc.n

    for s in (st_g[0], st_g[1]):
        ctx.wait("sync", s, s.n)
    if fused:
        ctx.pop_scope()
    return ctx


def host_prep_ffn0(inp):
    P = {}
    P["wgT"] = Tc(np.asarray(inp["L0_ffn_w_gate"], np.int8))
    P["wuT"] = Tc(np.asarray(inp["L0_ffn_w_up"], np.int8))
    P["wdT"] = Tc(np.asarray(inp["L0_ffn_w_down"], np.int8))
    return P


def maps_ffn0(P, Pf, r1):
    maps = []
    for c in range(8):
        b, h = c // 2, c % 2
        maps.append({
            "xn16i": np.ascontiguousarray(np.asarray(r1[c]["xn16"])[:, h * 2048:(h + 1) * 2048]),
            "wgT": Pf["wgT"], "wuT": Pf["wuT"], "wdT": Pf["wdT"], "cvec": P["cvec"],
        })
    return maps


class LNKit:

    def __init__(self, ctx, T, name="ln"):
        self.ctx = ctx
        self.T = T
        self.ps_s1 = ctx.psum(f"{name}_ps_s1", [1, T])
        self.ps_s2 = ctx.psum(f"{name}_ps_s2", [1, T])
        self.ps_b1 = ctx.psum(f"{name}_ps_b1", [128, T])
        self.ps_b2 = ctx.psum(f"{name}_ps_b2", [128, T])
        self.s_m = ctx.sbuf(f"{name}_m", [1, T], F32)
        self.s_msq = ctx.sbuf(f"{name}_msq", [1, T], F32)
        self.s_var = ctx.sbuf(f"{name}_var", [1, T], F32)
        self.s_sd = ctx.sbuf(f"{name}_sd", [1, T], F32)
        self.s_r = ctx.sbuf(f"{name}_r", [1, T], F32)
        self.s_sq = ctx.sbuf(f"{name}_sq", [128, KC * T], BF16)
        self.s_xb = ctx.sbuf(f"{name}_xb", [128, KC * T], BF16)
        self.k_stats = 0
        self.k_rep = 0
        self.k_norm = 0

    def emit(self, env, x_of, out_writes, eps_col, gcol=None, bcol=None, D_feat=1024):
        ctx, T = self.ctx, self.T
        vc, sc, tc = env["vc"], env["sc"], env["tc"]
        ones1f, ones128b = env["ones1f"], env["ones128b"]
        nk = D_feat // 128
        if self.k_stats:
            ctx.wait("vector", tc, self.k_stats)
        for k in range(nk):
            ctx.vec(lambda e, k=k: e.tensor_copy(self.s_xb[:, k * T:(k + 1) * T], x_of(k)))
            last = (k == nk - 1)
            v_cast = ctx.vec(lambda e, k=k: e.tensor_tensor(self.s_sq[:, k * T:(k + 1) * T], x_of(k), x_of(k), ALU.mult), vc if last else None)
        ctx.wait("tensor", vc, v_cast)

        def mm_stats(e):
            for k in range(nk):
                e.matmul(self.ps_s1[:, :], ones128b[:, :], self.s_xb[:, k * T:(k + 1) * T], start=(k == 0), stop=(k == nk - 1))
            for k in range(nk):
                ins = e.matmul(self.ps_s2[:, :], ones128b[:, :], self.s_sq[:, k * T:(k + 1) * T], start=(k == 0), stop=(k == nk - 1))
            return ins
        self.k_stats = ctx.pe(mm_stats, tc)
        ctx.wait("vector", tc, self.k_stats)
        ctx.vec(lambda e: e.tensor_scalar(self.s_m[:, :], self.ps_s1[:, :], 1.0 / D_feat, None, ALU.mult), vc)
        ctx.vedge("vector", vc)
        ctx.vec(lambda e: e.tensor_tensor(self.s_msq[:, :], self.s_m[:, :], self.s_m[:, :], ALU.mult), vc)
        ctx.vedge("vector", vc)
        v_var = ctx.vec(lambda e: e.scalar_tensor_tensor(self.s_var[:, :], self.ps_s2[:, :], 1.0 / D_feat, self.s_msq[:, :], ALU.mult, ALU.subtract), vc)
        ctx.wait("scalar", vc, v_var)
        v_sd = ctx.act(self.s_sd[:, :], self.s_var[:, :], AF.Sqrt, sem=sc, bias=eps_col)
        ctx.wait("vector", sc, v_sd)
        v_r = ctx.vec(lambda e: e.reciprocal(self.s_r[:, :], self.s_sd[:, :]), vc)
        ctx.wait("tensor", vc, v_r)
        if self.k_norm:
            ctx.wait("tensor", vc, self.k_norm)

        def mm_rep(e):
            e.matmul(self.ps_b1[:, :], ones1f[:, :], self.s_m[:, :], start=True, stop=True)
            return e.matmul(self.ps_b2[:, :], ones1f[:, :], self.s_r[:, :], start=True, stop=True)
        self.k_rep = ctx.pe(mm_rep, tc)
        ctx.wait("vector", tc, self.k_rep)
        v_last = 0
        for k in range(nk):
            v_last = out_writes(k)
        self.k_norm = v_last
        return v_last


def build_attn1(F=None):
    fused = F is not None
    if fused:
        ctx = F["ctx"]
        ctx.push_scope()
        _ph = "attn1"
        F["ext"].setdefault(_ph, {})

        def din(name, shape, dtype):
            if name in F["in"].get(_ph, {}):
                return F["in"][_ph][name]
            nm = _ph + "_" + name
            F["ext"][_ph][name] = nm
            return ctx.nc.dram_tensor(nm, shape, dtype, kind="ExternalInput").ap()

        def dout(name, shape, dtype):
            return F["out"][_ph][name]
    else:
        ctx = Ctx()
        din = ctx.din
        dout = ctx.dout
    mp0 = din("mp0", [D, 2048], F32)
    mp1 = None if fused else din("mp1", [D, 2048], F32)
    xn0h = din("xn0h", [D, L] if fused else [D, 2048], F32)
    qwT = din("qwT", [D, D], F32)
    kwT = din("kwT", [D, D], F32)
    qb = din("qb", [128, 8], F32)
    kb = din("kb", [128, 8], F32)
    wg = din("wg", [128, 8], BF16)
    ln0g = din("ln0g", [128, 8], F32)
    ln0b = din("ln0b", [128, 8], F32)
    ones1f = din("ones1f", [1, 128], F32)
    ones128b = din("ones128b", [128, 1], BF16)
    cvec = din("cvec", [128, 16], F32)

    o_h = dout("h0", [D, 2048], F32)
    o_k16 = dout("k16", [D, 2048], BF16)
    o_stats = dout("stats", [128, 9], F32)

    s_p0 = ctx.sbuf("s_p0", [128, KC * TT], F32)
    s_p1 = None if fused else ctx.sbuf("s_p1", [128, KC * TT], F32)
    s_xn = ctx.sbuf("s_xn", [128, KC * TT], F32)
    s_h = ctx.sbuf("s_h", [128, KC * TT], F32)
    s_hb = ctx.sbuf("s_hb", [128, KC * TT], BF16)
    s_lin = ctx.sbuf("s_lin", [128, KC * TT], F32)
    s_lsq = ctx.sbuf("s_lsq", [128, KC * TT], BF16)
    s_q16 = ctx.sbuf("s_q16", [128, KC * TT], BF16)
    s_k16 = ctx.sbuf("s_k16", [128, KC * TT], BF16)
    s_qwT = ctx.sbuf("s_qwT", [128, KC * D], BF16)
    s_kwT = ctx.sbuf("s_kwT", [128, KC * D], BF16)
    s_qb = ctx.sbuf("s_qb", [128, 8], F32)
    s_kb = ctx.sbuf("s_kb", [128, 8], F32)
    s_wg = ctx.sbuf("s_wg", [128, 8], BF16)
    s_g = ctx.sbuf("s_g", [128, 8], F32)
    s_bta = ctx.sbuf("s_bta", [128, 8], F32)
    s_ones1f = ctx.sbuf("s_ones1f", [1, 128], F32)
    s_ones128b = ctx.sbuf("s_ones128b", [128, 1], BF16)
    s_cvec = ctx.sbuf("s_cvec", [128, 16], F32)
    s_no = ctx.sbuf("s_no", [1, TT], F32)
    s_qw = ctx.sbuf("s_qw", [1, TT], F32)
    s_qsq = ctx.sbuf("s_qsq", [1, TT], F32)
    s_acc1 = ctx.sbuf("s_acc1", [1, 1], F32)
    s_red = ctx.sbuf("s_red", [1, 1], F32)
    s_gq = ctx.sbuf("s_gq", [128, 8], F32)
    s_gqt = ctx.sbuf("s_gqt", [128, 1], F32)
    s_stats = ctx.sbuf("s_stats", [128, 9], F32)

    kit = LNKit(ctx, TT, "ln0")
    ps_mm = [ctx.psum(f"ps_mm{i}", [128, TT]) for i in range(2)]
    ps_row = ctx.psum("ps_row", [1, TT])
    ps_qwr = ctx.psum("ps_qwr", [128, TT])

    ldc = ctx.sem("ldc")
    wld = ctx.sem("wld")
    ldt = ctx.sem("ldt")
    vc = ctx.sem("vc")
    sc = ctx.sem("sc")
    tc = ctx.sem("tc")
    st_h = ctx.sem("st_h")
    st_k = ctx.sem("st_k")
    st_s = ctx.sem("st_s")

    for t, d in ((s_qb, qb), (s_kb, kb), (s_wg, wg), (s_ones1f, ones1f),
                 (s_ones128b, ones128b), (s_cvec, cvec), (s_g, ln0g), (s_bta, ln0b)):
        ctx.dma("sync", t[:, :], d, ldc)
    v_ldc = ldc.n
    ctx.dma("gpsimd", r3(s_qwT[:, :], KC, D), fm_ap(qwT, D, 0, KC, D), wld)
    ctx.dma("gpsimd", r3(s_kwT[:, :], KC, D), fm_ap(kwT, D, 0, KC, D), wld)
    v_wld = wld.n

    ctx.wait("vector", ldc, v_ldc)
    ctx.wait("scalar", ldc, v_ldc)
    ctx.wait("tensor", ldc, v_ldc)
    ctx.wait("tensor", wld, v_wld)
    v_z1 = ctx.vec(lambda e: e.memset(s_gq[:, :], 0.0), vc)
    v_z2 = ctx.vec(lambda e: e.memset(s_acc1[:, :], 0.0), vc)
    v_z3 = ctx.vec(lambda e: e.memset(s_stats[:, :], 0.0), vc)

    env = {"vc": vc, "sc": sc, "tc": tc, "ones1f": s_ones1f, "ones128b": s_ones128b}

    NTI = 4
    k_hst = 0
    k_kst = 0
    k_qdone = 0
    k_kmmlast = 0
    kevac = [0, 0]
    k_row_read = 0
    k_qwr_read = 0
    k_qwmm = 0
    for it in range(NTI):
        if it >= 1:
            ctx.wait("sync", vc, k_qdone)
            ctx.wait("sync", tc, k_kmmlast)
        v_l0 = ctx.dma("sync", r3(s_p0[:, :], KC, TT), fm_ap(mp0, 2048, it * TT, KC, TT), ldt)
        if fused:
            v_l2 = F["dyn_dma"](ctx, r3(s_xn[:, :], KC, TT), xn0h, it * TT, [[L, 128], [L * 128, KC], [1, TT]], ldt)
        else:
            v_l1 = ctx.dma("sync", r3(s_p1[:, :], KC, TT), fm_ap(mp1, 2048, it * TT, KC, TT), ldt)
            v_l2 = ctx.dma("sync", r3(s_xn[:, :], KC, TT), fm_ap(xn0h, 2048, it * TT, KC, TT), ldt)
        ctx.wait("vector", ldt, v_l2)
        if it >= 1:
            ctx.wait("vector", tc, k_kmmlast)
            ctx.wait("vector", st_h, k_hst)
            ctx.wait("vector", st_k, k_kst)
            ctx.wait("vector", vc, k_qdone)

        if not fused:
            for k in range(KC):
                last = (k == KC - 1)
                ctx.vec(lambda e, k=k: e.tensor_tensor(s_p0[:, k * TT:(k + 1) * TT], s_p0[:, k * TT:(k + 1) * TT], s_p1[:, k * TT:(k + 1) * TT], ALU.add), vc if last else None)
            ctx.vedge("vector", vc)
        for k in range(KC):
            last = (k == KC - 1)
            ctx.vec(lambda e, k=k: e.tensor_tensor(s_p0[:, k * TT:(k + 1) * TT], s_p0[:, k * TT:(k + 1) * TT], s_xn[:, k * TT:(k + 1) * TT], ALU.add), vc if last else None)
        v_z = vc.n
        ctx.vedge("vector", vc)

        def out_writes(k):
            ctx.vec(lambda e, k=k: e.tensor_tensor(s_h[:, k * TT:(k + 1) * TT], s_p0[:, k * TT:(k + 1) * TT], kit.ps_b1[:, :], ALU.subtract), vc)
            ctx.vedge("vector", vc)
            ctx.vec(lambda e, k=k: e.tensor_tensor(s_h[:, k * TT:(k + 1) * TT], s_h[:, k * TT:(k + 1) * TT], kit.ps_b2[:, :], ALU.mult), vc)
            ctx.vedge("vector", vc)
            ctx.vec(lambda e, k=k: e.tensor_scalar(s_h[:, k * TT:(k + 1) * TT], s_h[:, k * TT:(k + 1) * TT], s_g[:, k:k + 1], s_bta[:, k:k + 1], ALU.mult, ALU.add), vc)
            ctx.vedge("vector", vc)
            return ctx.vec(lambda e, k=k: e.tensor_copy(s_hb[:, k * TT:(k + 1) * TT], s_h[:, k * TT:(k + 1) * TT]), vc)

        v_h = kit.emit(env, lambda k: s_p0[:, k * TT:(k + 1) * TT], out_writes, s_cvec[0:1, 1:2])

        ctx.wait("sync", vc, v_h)
        k_hst = ctx.dma("sync", fm_ap(o_h, 2048, it * TT, KC, TT), r3(s_h[:, :], KC, TT), st_h)

        for which in ("q", "k"):
            wT = s_qwT if which == "q" else s_kwT
            bias = s_qb if which == "q" else s_kb
            ctx.wait("tensor", vc, v_h)
            if which == "k":
                ctx.wait("vector", sc, sc.n)
            for oc in range(KC):
                pb = oc % 2
                ctx.wait("tensor", vc, kevac[pb])

                def mm_l(e, oc=oc, pb=pb, wT=wT):
                    for k in range(KC):
                        ins = e.matmul(ps_mm[pb][:, :], wT[:, k * D + oc * 128: k * D + (oc + 1) * 128], s_hb[:, k * TT:(k + 1) * TT], start=(k == 0), stop=(k == KC - 1))
                    return ins
                v_mm = ctx.pe(mm_l, tc)
                if which == "k" and oc == KC - 1:
                    k_kmmlast = v_mm
                ctx.wait("vector", tc, v_mm)
                v_ev = ctx.vec(lambda e, oc=oc, pb=pb, bias=bias: e.tensor_scalar(s_lin[:, oc * TT:(oc + 1) * TT], ps_mm[pb][:, :], bias[:, oc:oc + 1], None, ALU.add), vc)
                kevac[pb] = v_ev
            v_lind = vc.n
            ctx.vedge("vector", vc)
            for k in range(KC):
                last = (k == KC - 1)
                ctx.vec(lambda e, k=k: e.tensor_tensor(s_lsq[:, k * TT:(k + 1) * TT], s_lin[:, k * TT:(k + 1) * TT], s_lin[:, k * TT:(k + 1) * TT], ALU.mult), vc if last else None)
            v_sq = vc.n
            ctx.wait("tensor", vc, v_sq)
            ctx.wait("tensor", sc, k_row_read)

            def mm_ss(e):
                for k in range(KC):
                    ins = e.matmul(ps_row[:, :], s_ones128b[:, :], s_lsq[:, k * TT:(k + 1) * TT], start=(k == 0), stop=(k == KC - 1))
                return ins
            v_ssmm = ctx.pe(mm_ss, tc)
            ctx.wait("scalar", tc, v_ssmm)
            v_nsd = ctx.act(s_no[:, :], ps_row[:, :], AF.Sqrt, sem=sc)
            k_row_read = v_nsd
            ctx.wait("vector", sc, v_nsd)
            ctx.vec(lambda e: e.tensor_scalar(s_no[:, :], s_no[:, :], s_cvec[0:1, 1:2], None, ALU.max), vc)
            ctx.vedge("vector", vc)
            v_rec = ctx.vec(lambda e: e.reciprocal(s_no[:, :], s_no[:, :]), vc)
            ctx.wait("tensor", vc, v_rec)
            ctx.wait("tensor", vc, k_qwr_read)
            v_bc = ctx.pe(lambda e: e.matmul(ps_qwr[:, :], s_ones1f[:, :], s_no[:, :], start=True, stop=True), tc)
            ctx.wait("vector", tc, v_bc)
            dst = s_q16 if which == "q" else s_k16
            for k in range(KC):
                v_n = ctx.vec(lambda e, k=k: e.tensor_tensor(s_lin[:, k * TT:(k + 1) * TT], s_lin[:, k * TT:(k + 1) * TT], ps_qwr[:, :], ALU.mult), vc)
                k_qwr_read = v_n
                if which == "q":
                    ctx.wait("scalar", vc, v_n)
                    if it >= 1 and k == 0:
                        ctx.wait("scalar", tc, k_qwmm)
                        ctx.wait("scalar", vc, k_qdone)
                    ctx.act(dst[:, k * TT:(k + 1) * TT], s_lin[:, k * TT:(k + 1) * TT], AF.Gelu, sem=sc)
                else:
                    ctx.vedge("vector", vc)
                    ctx.vec(lambda e, k=k, dst=dst: e.tensor_copy(dst[:, k * TT:(k + 1) * TT], s_lin[:, k * TT:(k + 1) * TT]), vc)
            if which == "q":
                v_qgelu = sc.n
            else:
                v_kdone = vc.n
        ctx.wait("sync", vc, v_kdone)
        k_kst = ctx.dma("sync", fm_ap(o_k16, 2048, it * TT, KC, TT), r3(s_k16[:, :], KC, TT), st_k)

        ctx.wait("tensor", sc, v_qgelu)

        def mm_qw(e):
            for k in range(KC):
                ins = e.matmul(ps_row[:, :], s_wg[:, k:k + 1], s_q16[:, k * TT:(k + 1) * TT], start=(k == 0), stop=(k == KC - 1))
            return ins
        v_qwmm = ctx.pe(mm_qw, tc)
        k_qwmm = v_qwmm
        ctx.wait("vector", tc, v_qwmm)
        v_qw = ctx.vec(lambda e: e.tensor_scalar(s_qw[:, :], ps_row[:, :], s_cvec[0:1, 7:8], None, ALU.mult), vc)
        ctx.vedge("vector", vc)
        ctx.vec(lambda e: e.tensor_tensor(s_qsq[:, :], s_qw[:, :], s_qw[:, :], ALU.mult), vc)
        ctx.vedge("vector", vc)
        ctx.vec(lambda e: e.reduce_sum(s_red[:, :], s_qsq[:, :], axis=AX), vc)
        ctx.vedge("vector", vc)
        ctx.vec(lambda e: e.tensor_tensor(s_acc1[:, :], s_acc1[:, :], s_red[:, :], ALU.add), vc)
        ctx.wait("tensor", vc, v_qw)
        v_qwr = ctx.pe(lambda e: e.matmul(ps_qwr[:, :], s_ones1f[:, :], s_qw[:, :], start=True, stop=True), tc)
        ctx.wait("vector", tc, v_qwr)
        for k in range(KC):
            ctx.vec(lambda e, k=k: e.tensor_tensor(s_lin[:, k * TT:(k + 1) * TT], s_q16[:, k * TT:(k + 1) * TT], ps_qwr[:, :], ALU.mult), vc)
            ctx.vedge("vector", vc)
            ctx.vec(lambda e, k=k: e.reduce_sum(s_gqt[:, :], s_lin[:, k * TT:(k + 1) * TT], axis=AX), vc)
            ctx.vedge("vector", vc)
            v_gq = ctx.vec(lambda e, k=k: e.tensor_tensor(s_gq[:, k:k + 1], s_gq[:, k:k + 1], s_gqt[:, :], ALU.add), vc)
            ctx.vedge("vector", vc)
        k_qdone = vc.n

    ctx.vedge("vector", vc)
    ctx.vec(lambda e: e.tensor_copy(s_stats[:, 0:8], s_gq[:, :]), vc)
    ctx.vedge("vector", vc)
    v_fin = ctx.vec(lambda e: e.tensor_copy(s_stats[0:1, 8:9], s_acc1[:, :]), vc)
    ctx.wait("sync", vc, v_fin)
    ctx.dma("sync", o_stats, s_stats[:, :], st_s)
    for s in (st_h, st_k, st_s):
        ctx.wait("sync", s, s.n)
    if fused:
        ctx.pop_scope()
    return ctx


def host_prep_attn1(inp):
    f32 = np.float32
    P = {}
    P["qwT"] = Tc(np.asarray(inp["L0_attn_q_w"], f32))
    P["kwT"] = Tc(np.asarray(inp["L0_attn_k_w"], f32))
    P["qb"] = pmaj(np.asarray(inp["L0_attn_q_b"], f32))
    P["kb"] = pmaj(np.asarray(inp["L0_attn_k_b"], f32))
    P["wg"] = pmaj(np.asarray(inp["L0_attn_w_g"], f32)[:, 0]).astype(NB16)
    P["ln0g"] = pmaj(np.asarray(inp["L0_attn_ln0_g"], f32))
    P["ln0b"] = pmaj(np.asarray(inp["L0_attn_ln0_b"], f32))
    return P


def build_attn2(F=None):
    fused = F is not None
    if fused:
        ctx = F["ctx"]
        ctx.push_scope()
        _ph = "attn2"
        F["ext"].setdefault(_ph, {})

        def din(name, shape, dtype):
            if name in F["in"].get(_ph, {}):
                return F["in"][_ph][name]
            nm = _ph + "_" + name
            F["ext"][_ph][name] = nm
            return ctx.nc.dram_tensor(nm, shape, dtype, kind="ExternalInput").ap()

        def dout(name, shape, dtype):
            return F["out"][_ph][name]
    else:
        ctx = Ctx()
        din = ctx.din
        dout = ctx.dout
    st0 = din("st0", [128, 9], F32)
    st1 = None if fused else din("st1", [128, 9], F32)
    h0 = din("h0i", [D, 2048], F32)
    k16 = din("k16i", [D, 2048], BF16)
    projT = din("projT", [D, D], F32)
    flT = din("flT", [D, D], F32)
    pb_ = din("pb", [128, 8], F32)
    fb_ = din("fb", [128, 8], F32)
    lnig = din("lnig", [128, 8], F32)
    lnib = din("lnib", [128, 8], F32)
    ln1g = din("ln1g", [128, 8], F32)
    ln1b = din("ln1b", [128, 8], F32)
    ones1f = din("ones1f", [1, 128], F32)
    ones128b = din("ones128b", [128, 1], BF16)
    cvec = din("cvec", [128, 16], F32)

    o_h2 = dout("h2", [D, 2048], F32)
    o_h2b = dout("h2b", [D, 2048], BF16)

    s_h = ctx.sbuf("s_h", [128, KC * TT], F32)
    s_k = ctx.sbuf("s_k", [128, KC * TT], BF16)
    s_gk = ctx.sbuf("s_gk", [128, KC * TT], BF16)
    s_hs = ctx.sbuf("s_hs", [128, KC * TT], F32)
    s_mid = ctx.sbuf("s_mid", [128, KC * TT], BF16)
    s_z2 = ctx.sbuf("s_z2", [128, KC * TT], F32)
    s_h2 = ctx.sbuf("s_h2", [128, KC * TT], F32)
    s_h2b = ctx.sbuf("s_h2b", [128, KC * TT], BF16)
    s_t = ctx.sbuf("s_t", [128, TT], F32)
    s_projT = ctx.sbuf("s_projT", [128, KC * D], BF16)
    s_flT = ctx.sbuf("s_flT", [128, KC * D], BF16)
    s_pb = ctx.sbuf("s_pb", [128, 8], F32)
    s_fb = ctx.sbuf("s_fb", [128, 8], F32)
    s_lnig = ctx.sbuf("s_lnig", [128, 8], F32)
    s_lnib = ctx.sbuf("s_lnib", [128, 8], F32)
    s_ln1g = ctx.sbuf("s_ln1g", [128, 8], F32)
    s_ln1b = ctx.sbuf("s_ln1b", [128, 8], F32)
    s_st0 = ctx.sbuf("s_st0", [128, 9], F32)
    s_st1 = None if fused else ctx.sbuf("s_st1", [128, 9], F32)
    s_G = ctx.sbuf("s_G", [128, 8], F32)
    s_sc1 = ctx.sbuf("s_sc1", [1, 1], F32)
    s_ones1f = ctx.sbuf("s_ones1f", [1, 128], F32)
    s_ones128b = ctx.sbuf("s_ones128b", [128, 1], BF16)
    s_cvec = ctx.sbuf("s_cvec", [128, 16], F32)

    kit = LNKit(ctx, TT, "l5")
    ps_mm = [ctx.psum(f"ps_mm{i}", [128, TT]) for i in range(2)]

    ldc = ctx.sem("ldc")
    wld = ctx.sem("wld")
    ldt = ctx.sem("ldt")
    vc = ctx.sem("vc")
    sc = ctx.sem("sc")
    tc = ctx.sem("tc")
    st_h2 = ctx.sem("st_h2")
    st_h2b = ctx.sem("st_h2b")

    pairs = [(s_st0, st0), (s_pb, pb_), (s_fb, fb_), (s_lnig, lnig),
             (s_lnib, lnib), (s_ln1g, ln1g), (s_ln1b, ln1b), (s_ones1f, ones1f),
             (s_ones128b, ones128b), (s_cvec, cvec)]
    if not fused:
        pairs.insert(1, (s_st1, st1))
    for t, d in pairs:
        ctx.dma("sync", t[:, :], d, ldc)
    v_ldc = ldc.n
    ctx.dma("gpsimd", r3(s_projT[:, :], KC, D), fm_ap(projT, D, 0, KC, D), wld)
    ctx.dma("gpsimd", r3(s_flT[:, :], KC, D), fm_ap(flT, D, 0, KC, D), wld)
    v_wld = wld.n
    ctx.wait("vector", ldc, v_ldc)
    ctx.wait("scalar", ldc, v_ldc)
    ctx.wait("tensor", ldc, v_ldc)
    ctx.wait("tensor", wld, v_wld)

    if fused:
        ctx.vec(lambda e: e.tensor_copy(s_G[:, :], s_st0[:, 0:8]), vc)
        v_ssq = ctx.vec(lambda e: e.tensor_copy(s_sc1[:, :], s_st0[0:1, 8:9]), vc)
    else:
        ctx.vec(lambda e: e.tensor_tensor(s_G[:, :], s_st0[:, 0:8], s_st1[:, 0:8], ALU.add), vc)
        v_ssq = ctx.vec(lambda e: e.tensor_tensor(s_sc1[:, :], s_st0[0:1, 8:9], s_st1[0:1, 8:9], ALU.add), vc)
    ctx.wait("scalar", vc, v_ssq)
    v_sq = ctx.act(s_sc1[:, :], s_sc1[:, :], AF.Sqrt, sem=sc)
    ctx.wait("vector", sc, v_sq)
    ctx.vec(lambda e: e.tensor_scalar(s_sc1[:, :], s_sc1[:, :], s_cvec[0:1, 1:2], None, ALU.max), vc)
    ctx.vedge("vector", vc)
    v_rno = ctx.vec(lambda e: e.reciprocal(s_sc1[:, :], s_sc1[:, :]), vc)
    ctx.wait("tensor", vc, v_rno)
    v_gbc = ctx.pe(lambda e: e.matmul(ps_mm[0][:, :128 // 128], s_ones1f[:, :], s_sc1[:, :], start=True, stop=True), tc)
    ctx.wait("vector", tc, v_gbc)
    v_G = ctx.vec(lambda e: e.tensor_scalar(s_G[:, :], s_G[:, :], ps_mm[0][:, 0:1], None, ALU.mult), vc)

    env = {"vc": vc, "sc": sc, "tc": tc, "ones1f": s_ones1f, "ones128b": s_ones128b}

    NTI = 4
    kevac = [v_G, v_G]
    k_last_mm = 0
    k_h2st = 0
    k_h2bst = 0
    k_middone = 0
    for it in range(NTI):
        if it >= 1:
            ctx.wait("sync", vc, k_middone)
            ctx.wait("sync", tc, k_last_mm)
            ctx.wait("sync", st_h2, k_h2st)
            ctx.wait("sync", st_h2b, k_h2bst)
        v_lh = ctx.dma("sync", r3(s_h[:, :], KC, TT), fm_ap(h0, 2048, it * TT, KC, TT), ldt)
        v_lk = ctx.dma("sync", r3(s_k[:, :], KC, TT), fm_ap(k16, 2048, it * TT, KC, TT), ldt)
        ctx.wait("vector", ldt, v_lk)
        if it == 0:
            ctx.wait("vector", vc, v_G)

        for k in range(KC):
            last = (k == KC - 1)
            ctx.vec(lambda e, k=k: e.tensor_scalar(s_gk[:, k * TT:(k + 1) * TT], s_k[:, k * TT:(k + 1) * TT], s_G[:, k:k + 1], None, ALU.mult), vc if last else None)
        v_gk = vc.n

        ctx.wait("tensor", vc, v_gk)
        for oc in range(KC):
            pbk = oc % 2
            ctx.wait("tensor", vc, kevac[pbk])

            def mm_p(e, oc=oc, pbk=pbk):
                for k in range(KC):
                    ins = e.matmul(ps_mm[pbk][:, :], s_projT[:, k * D + oc * 128: k * D + (oc + 1) * 128], s_gk[:, k * TT:(k + 1) * TT], start=(k == 0), stop=(k == KC - 1))
                return ins
            v_mm = ctx.pe(mm_p, tc)
            ctx.wait("vector", tc, v_mm)
            kevac[pbk] = ctx.vec(lambda e, oc=oc, pbk=pbk: e.tensor_scalar(s_hs[:, oc * TT:(oc + 1) * TT], ps_mm[pbk][:, :], s_pb[:, oc:oc + 1], None, ALU.add), vc)

        def out_writes(k):
            ctx.vedge("vector", vc)
            ctx.vec(lambda e, k=k: e.tensor_tensor(s_t[:, :], s_h[:, k * TT:(k + 1) * TT], kit.ps_b1[:, :], ALU.subtract), vc)
            ctx.vedge("vector", vc)
            ctx.vec(lambda e, k=k: e.tensor_tensor(s_t[:, :], s_t[:, :], kit.ps_b2[:, :], ALU.mult), vc)
            ctx.vedge("vector", vc)
            ctx.vec(lambda e, k=k: e.tensor_scalar(s_t[:, :], s_t[:, :], s_lnig[:, k:k + 1], s_lnib[:, k:k + 1], ALU.mult, ALU.add), vc)
            ctx.vedge("vector", vc)
            return ctx.vec(lambda e, k=k: e.tensor_tensor(s_mid[:, k * TT:(k + 1) * TT], s_hs[:, k * TT:(k + 1) * TT], s_t[:, :], ALU.mult), vc)

        ctx.vedge("vector", vc)
        v_mid = kit.emit(env, lambda k: s_h[:, k * TT:(k + 1) * TT], out_writes, s_cvec[0:1, 1:2])
        k_middone = v_mid

        ctx.wait("tensor", vc, v_mid)
        for oc in range(KC):
            pbk = oc % 2
            ctx.wait("tensor", vc, kevac[pbk])

            def mm_f(e, oc=oc, pbk=pbk):
                for k in range(KC):
                    ins = e.matmul(ps_mm[pbk][:, :], s_flT[:, k * D + oc * 128: k * D + (oc + 1) * 128], s_mid[:, k * TT:(k + 1) * TT], start=(k == 0), stop=(k == KC - 1))
                return ins
            v_mm = ctx.pe(mm_f, tc)
            k_last_mm = v_mm
            ctx.wait("vector", tc, v_mm)
            kevac[pbk] = ctx.vec(lambda e, oc=oc, pbk=pbk: e.scalar_tensor_tensor(s_z2[:, oc * TT:(oc + 1) * TT], ps_mm[pbk][:, :], s_fb[:, oc:oc + 1], s_h[:, oc * TT:(oc + 1) * TT], ALU.add, ALU.add), vc)

        def out_writes2(k):
            ctx.vec(lambda e, k=k: e.tensor_tensor(s_h2[:, k * TT:(k + 1) * TT], s_z2[:, k * TT:(k + 1) * TT], kit.ps_b1[:, :], ALU.subtract), vc)
            ctx.vedge("vector", vc)
            ctx.vec(lambda e, k=k: e.tensor_tensor(s_h2[:, k * TT:(k + 1) * TT], s_h2[:, k * TT:(k + 1) * TT], kit.ps_b2[:, :], ALU.mult), vc)
            ctx.vedge("vector", vc)
            ctx.vec(lambda e, k=k: e.tensor_scalar(s_h2[:, k * TT:(k + 1) * TT], s_h2[:, k * TT:(k + 1) * TT], s_ln1g[:, k:k + 1], s_ln1b[:, k:k + 1], ALU.mult, ALU.add), vc)
            ctx.vedge("vector", vc)
            return ctx.vec(lambda e, k=k: e.tensor_copy(s_h2b[:, k * TT:(k + 1) * TT], s_h2[:, k * TT:(k + 1) * TT]), vc)

        ctx.vedge("vector", vc)
        v_h2 = kit.emit(env, lambda k: s_z2[:, k * TT:(k + 1) * TT], out_writes2, s_cvec[0:1, 1:2])

        ctx.wait("sync", vc, v_h2)
        k_h2st = ctx.dma("sync", fm_ap(o_h2, 2048, it * TT, KC, TT), r3(s_h2[:, :], KC, TT), st_h2)
        k_h2bst = ctx.dma("sync", fm_ap(o_h2b, 2048, it * TT, KC, TT), r3(s_h2b[:, :], KC, TT), st_h2b)

    for s in (st_h2, st_h2b):
        ctx.wait("sync", s, s.n)
    if fused:
        ctx.pop_scope()
    return ctx


def build_comb(F=None):
    fused = F is not None
    if fused:
        ctx = F["ctx"]
        ctx.push_scope()
        _ph = "comb"
        F["ext"].setdefault(_ph, {})

        def din(name, shape, dtype):
            if name in F["in"].get(_ph, {}):
                return F["in"][_ph][name]
            nm = _ph + "_" + name
            F["ext"][_ph][name] = nm
            return ctx.nc.dram_tensor(nm, shape, dtype, kind="ExternalInput").ap()

        def dout(name, shape, dtype):
            return F["out"][_ph][name]
    else:
        ctx = Ctx()
        din = ctx.din
        dout = ctx.dout
    h2 = din("h2i", [D, 2048], F32)
    h2b = din("h2bi", [D, 2048], BF16)
    x0h = din("x0h", [D, 2048], F32)
    g0 = din("g0i", [D, 2048], BF16)
    d1T = din("d1T", [D, 4 * D], F32)
    d2T = din("d2T", [4 * D, D], F32)
    d1b = din("d1b", [128, 32], F32)
    d2b = din("d2b", [128, 8], F32)
    ln2g = din("ln2g", [128, 8], F32)
    ln2b = din("ln2b", [128, 8], F32)
    ones1f = din("ones1f", [1, 128], F32)
    ones128b = din("ones128b", [128, 1], BF16)
    cvec = din("cvec", [128, 16], F32)

    o_x1 = dout("x1", [D, 2048], F32)
    o_xn1 = dout("xn1", [D, 2048], F32)
    o_xn1b = dout("xn1b", [D, 2048], BF16)

    s_h2 = ctx.sbuf("s_h2", [128, KC * TT], F32)
    s_h2b = ctx.sbuf("s_h2b", [128, KC * TT], BF16)
    s_f1 = ctx.sbuf("s_f1", [128, 32 * TT], BF16)
    s_fa = ctx.sbuf("s_fa", [128, KC * TT], F32)
    s_x0 = ctx.sbuf("s_x0", [128, KC * TT], F32)
    s_g0 = ctx.sbuf("s_g0", [128, KC * TT], BF16)
    s_x1 = ctx.sbuf("s_x1", [128, KC * TT], F32)
    s_xn1 = ctx.sbuf("s_xn1", [128, KC * TT], F32)
    s_xn1b = ctx.sbuf("s_xn1b", [128, KC * TT], BF16)
    s_d1 = [ctx.sbuf(f"s_d1_{i}", [128, KC * 128], BF16) for i in range(2)]
    s_d2 = [ctx.sbuf(f"s_d2_{i}", [128, KC * D], BF16) for i in range(2)]
    s_d1b = ctx.sbuf("s_d1b", [128, 32], F32)
    s_d2b = ctx.sbuf("s_d2b", [128, 8], F32)
    s_ln2g = ctx.sbuf("s_ln2g", [128, 8], F32)
    s_ln2b = ctx.sbuf("s_ln2b", [128, 8], F32)
    s_ones1f = ctx.sbuf("s_ones1f", [1, 128], F32)
    s_ones128b = ctx.sbuf("s_ones128b", [128, 1], BF16)
    s_cvec = ctx.sbuf("s_cvec", [128, 16], F32)

    kit = LNKit(ctx, TT, "l5b")
    ps_mm = [ctx.psum(f"ps_mm{i}", [128, TT]) for i in range(2)]

    ldc = ctx.sem("ldc")
    ldt = ctx.sem("ldt")
    lw1 = [ctx.sem("lw1_0"), ctx.sem("lw1_1")]
    lw2 = [ctx.sem("lw2_0"), ctx.sem("lw2_1")]
    vc = ctx.sem("vc")
    sc = ctx.sem("sc")
    tc = ctx.sem("tc")
    st_x1 = ctx.sem("st_x1")
    st_xn = ctx.sem("st_xn")
    st_xnb = ctx.sem("st_xnb")

    for t, d in ((s_d1b, d1b), (s_d2b, d2b), (s_ln2g, ln2g), (s_ln2b, ln2b),
                 (s_ones1f, ones1f), (s_ones128b, ones128b), (s_cvec, cvec)):
        ctx.dma("sync", t[:, :], d, ldc)
    v_ldc = ldc.n
    ctx.wait("vector", ldc, v_ldc)
    ctx.wait("scalar", ldc, v_ldc)
    ctx.wait("tensor", ldc, v_ldc)

    env = {"vc": vc, "sc": sc, "tc": tc, "ones1f": s_ones1f, "ones128b": s_ones128b}

    NTI = 4
    k_x1st = 0
    k_xnst = 0
    k_xnbst = 0
    k_lastf1mm = 0
    k_gelu = [0, 0]
    k_d1used = [0, 0]
    k_d2used = [0, 0]
    k_faccdone = 0
    k_s0done = 0
    for it in range(NTI):
        if it >= 1:
            ctx.wait("sync", vc, k_s0done)
            ctx.wait("sync", st_x1, k_x1st)
            ctx.wait("sync", st_xn, k_xnst)
            ctx.wait("sync", st_xnb, k_xnbst)
        v_lh = ctx.dma("sync", r3(s_h2[:, :], KC, TT), fm_ap(h2, 2048, it * TT, KC, TT), ldt)
        v_lhb = ctx.dma("sync", r3(s_h2b[:, :], KC, TT), fm_ap(h2b, 2048, it * TT, KC, TT), ldt)
        v_lx0 = ctx.dma("sync", r3(s_x0[:, :], KC, TT), fm_ap(x0h, 2048, it * TT, KC, TT), ldt)
        v_lg0 = ctx.dma("sync", r3(s_g0[:, :], KC, TT), fm_ap(g0, 2048, it * TT, KC, TT), ldt)
        ctx.wait("tensor", ldt, v_lg0)

        for oc in range(32):
            wb = oc % 2
            gi = it * 32 + oc
            if gi >= 2:
                ctx.wait("gpsimd", tc, k_d1used[wb])
            v_w = ctx.dma("gpsimd", r3(s_d1[wb][:, :], KC, 128), fm_ap(d1T, 4 * D, oc * 128, KC, 128), lw1[wb])
            ctx.wait("tensor", lw1[wb], v_w)
            pbk = oc % 2
            if gi >= 2:
                ctx.wait("tensor", sc, k_gelu[pbk])

            def mm_1(e, oc=oc, wb=wb, pbk=pbk):
                for k in range(KC):
                    ins = e.matmul(ps_mm[pbk][:, :], s_d1[wb][:, k * 128:(k + 1) * 128], s_h2b[:, k * TT:(k + 1) * TT], start=(k == 0), stop=(k == KC - 1))
                return ins
            v_mm = ctx.pe(mm_1, tc)
            k_d1used[wb] = v_mm
            k_lastf1mm = v_mm
            ctx.wait("scalar", tc, v_mm)
            if it >= 1 and oc == 0:
                ctx.wait("scalar", tc, k_faccdone)
            k_gelu[pbk] = ctx.act(s_f1[:, oc * TT:(oc + 1) * TT], ps_mm[pbk][:, :], AF.Gelu, sem=sc, bias=s_d1b[:, oc:oc + 1])
        v_geludone = sc.n

        ctx.wait("tensor", sc, v_geludone)
        for kg in range(4):
            wb = kg % 2
            gi2 = it * 4 + kg
            if gi2 >= 2:
                ctx.wait("gpsimd", tc, k_d2used[wb])
            v_w2 = ctx.dma("gpsimd", r3(s_d2[wb][:, :], KC, D), fm_ap(d2T, D, 0, KC, D, row0=kg * 1024), lw2[wb])
            ctx.wait("tensor", lw2[wb], v_w2)
            for oc in range(KC):
                pbk = oc % 2
                if (gi2 * KC + oc) >= 2:
                    ctx.wait("tensor", vc, kfe[pbk])

                def mm_2(e, oc=oc, wb=wb, pbk=pbk, kg=kg):
                    for k in range(KC):
                        ins = e.matmul(ps_mm[pbk][:, :], s_d2[wb][:, k * D + oc * 128: k * D + (oc + 1) * 128], s_f1[:, (kg * KC + k) * TT:(kg * KC + k + 1) * TT], start=(k == 0), stop=(k == KC - 1))
                    return ins
                v_mm = ctx.pe(mm_2, tc)
                k_d2used[wb] = v_mm
                ctx.wait("vector", tc, v_mm)
                if kg == 0:
                    v_fe = ctx.vec(lambda e, oc=oc, pbk=pbk: e.tensor_copy(s_fa[:, oc * TT:(oc + 1) * TT], ps_mm[pbk][:, :]), vc)
                else:
                    v_fe = ctx.vec(lambda e, oc=oc, pbk=pbk: e.tensor_tensor(s_fa[:, oc * TT:(oc + 1) * TT], s_fa[:, oc * TT:(oc + 1) * TT], ps_mm[pbk][:, :], ALU.add), vc)
                if gi2 == 0 and oc == 0:
                    kfe = [0, 0]
                kfe[pbk] = v_fe
        k_faccdone = tc.n
        ctx.vedge("vector", vc)
        for k in range(KC):
            last = (k == KC - 1)
            ctx.vec(lambda e, k=k: e.scalar_tensor_tensor(s_fa[:, k * TT:(k + 1) * TT], s_fa[:, k * TT:(k + 1) * TT], s_d2b[:, k:k + 1], s_h2[:, k * TT:(k + 1) * TT], ALU.add, ALU.add), vc if last else None)
        ctx.vedge("vector", vc)

        def out_writes(k):
            ctx.vec(lambda e, k=k: e.tensor_tensor(s_x1[:, k * TT:(k + 1) * TT], s_fa[:, k * TT:(k + 1) * TT], kit.ps_b1[:, :], ALU.subtract), vc)
            ctx.vedge("vector", vc)
            ctx.vec(lambda e, k=k: e.tensor_tensor(s_x1[:, k * TT:(k + 1) * TT], s_x1[:, k * TT:(k + 1) * TT], kit.ps_b2[:, :], ALU.mult), vc)
            ctx.vedge("vector", vc)
            return ctx.vec(lambda e, k=k: e.tensor_scalar(s_x1[:, k * TT:(k + 1) * TT], s_x1[:, k * TT:(k + 1) * TT], s_ln2g[:, k:k + 1], s_ln2b[:, k:k + 1], ALU.mult, ALU.add), vc)

        kit_v = kit.emit(env, lambda k: s_fa[:, k * TT:(k + 1) * TT], out_writes, s_cvec[0:1, 1:2])

        ctx.wait("vector", ldt, v_lg0)
        ctx.vedge("vector", vc)
        for k in range(KC):
            ctx.vec(lambda e, k=k: e.tensor_tensor(s_x1[:, k * TT:(k + 1) * TT], s_x1[:, k * TT:(k + 1) * TT], s_g0[:, k * TT:(k + 1) * TT], ALU.mult), vc)
            ctx.vedge("vector", vc)
            last = (k == KC - 1)
            ctx.vec(lambda e, k=k: e.tensor_tensor(s_x1[:, k * TT:(k + 1) * TT], s_x1[:, k * TT:(k + 1) * TT], s_x0[:, k * TT:(k + 1) * TT], ALU.add), vc if True else None)
        v_x1 = vc.n
        ctx.wait("sync", vc, v_x1)
        k_x1st = ctx.dma("sync", fm_ap(o_x1, 2048, it * TT, KC, TT), r3(s_x1[:, :], KC, TT), st_x1)

        def out_writes3(k):
            ctx.vec(lambda e, k=k: e.tensor_tensor(s_xn1[:, k * TT:(k + 1) * TT], s_x1[:, k * TT:(k + 1) * TT], kit.ps_b1[:, :], ALU.subtract), vc)
            ctx.vedge("vector", vc)
            ctx.vec(lambda e, k=k: e.tensor_tensor(s_xn1[:, k * TT:(k + 1) * TT], s_xn1[:, k * TT:(k + 1) * TT], kit.ps_b2[:, :], ALU.mult), vc)
            ctx.vedge("vector", vc)
            return ctx.vec(lambda e, k=k: e.tensor_copy(s_xn1b[:, k * TT:(k + 1) * TT], s_xn1[:, k * TT:(k + 1) * TT]), vc)

        ctx.vedge("vector", vc)
        v_xn = kit.emit(env, lambda k: s_x1[:, k * TT:(k + 1) * TT], out_writes3, s_cvec[0:1, 0:1])
        k_s0done = v_xn
        ctx.wait("sync", vc, v_xn)
        k_xnst = ctx.dma("sync", fm_ap(o_xn1, 2048, it * TT, KC, TT), r3(s_xn1[:, :], KC, TT), st_xn)
        k_xnbst = ctx.dma("sync", fm_ap(o_xn1b, 2048, it * TT, KC, TT), r3(s_xn1b[:, :], KC, TT), st_xnb)

    for s in (st_x1, st_xn, st_xnb):
        ctx.wait("sync", s, s.n)
    if fused:
        ctx.pop_scope()
    return ctx


def host_prep_attn2(inp):
    f32 = np.float32
    P = {}
    P["projT"] = Tc(np.asarray(inp["L0_attn_proj_w"], f32))
    P["flT"] = Tc(np.asarray(inp["L0_attn_fl_w"], f32))
    P["pb"] = pmaj(np.asarray(inp["L0_attn_proj_b"], f32))
    P["fb"] = pmaj(np.asarray(inp["L0_attn_fl_b"], f32))
    P["lnig"] = pmaj(np.asarray(inp["L0_attn_lni_g"], f32))
    P["lnib"] = pmaj(np.asarray(inp["L0_attn_lni_b"], f32))
    P["ln1g"] = pmaj(np.asarray(inp["L0_attn_ln1_g"], f32))
    P["ln1b"] = pmaj(np.asarray(inp["L0_attn_ln1_b"], f32))
    P["d1T"] = Tc(np.asarray(inp["L0_attn_d1_w"], f32))
    P["d2T"] = Tc(np.asarray(inp["L0_attn_d2_w"], f32))
    P["d1b"] = pmaj(np.asarray(inp["L0_attn_d1_b"], f32))
    P["d2b"] = pmaj(np.asarray(inp["L0_attn_d2_b"], f32))
    P["ln2g"] = pmaj(np.asarray(inp["L0_attn_ln2_g"], f32))
    P["ln2b"] = pmaj(np.asarray(inp["L0_attn_ln2_b"], f32))
    return P


def build_qkv(F=None):
    fused = F is not None
    if fused:
        ctx = F["ctx"]
        ctx.push_scope()
        _ph = "qkv"
        F["ext"].setdefault(_ph, {})

        def din(name, shape, dtype):
            if name in F["in"].get(_ph, {}):
                return F["in"][_ph][name]
            nm = _ph + "_" + name
            F["ext"][_ph][name] = nm
            return ctx.nc.dram_tensor(nm, shape, dtype, kind="ExternalInput").ap()

        def dout(name, shape, dtype):
            return F["out"][_ph][name]
    else:
        ctx = Ctx()
        din = ctx.din
        dout = ctx.dout
    xn1b = din("xn1bi", [D, 2048], BF16)
    WqT = din("WqT", [D, D], F32)
    WkT = din("WkT", [D, D], F32)
    WvT = din("WvT", [D, D], F32)
    bq = din("bq", [128, 8], F32)
    bk = din("bk", [128, 8], F32)
    bv = din("bv", [128, 8], F32)
    o_q = dout("q1", [D, 2048], BF16)
    if fused:
        o_k = dout("kv", [D, 4096], BF16)
        o_v = o_k
    else:
        o_k = dout("k1", [D, 2048], BF16)
        o_v = dout("v1", [D, 2048], BF16)

    s_x = [ctx.sbuf(f"s_x{i}", [128, KC * TT], BF16) for i in range(2)]
    s_w = [ctx.sbuf(f"s_w{i}", [128, KC * D], BF16) for i in range(3)]
    s_b = ctx.sbuf("s_b", [128, 24], F32)
    s_o = [ctx.sbuf(f"s_o{i}", [128, TT], BF16) for i in range(2)]

    ps = [ctx.psum(f"ps{i}", [128, TT]) for i in range(2)]

    ldc = ctx.sem("ldc")
    wld = ctx.sem("wld")
    ldx = [ctx.sem("ldx0"), ctx.sem("ldx1")]
    vc = ctx.sem("vc")
    tc = ctx.sem("tc")
    st = [ctx.sem("st0"), ctx.sem("st1")]

    ctx.dma("sync", s_b[:, 0:8], bq, ldc)
    ctx.dma("sync", s_b[:, 8:16], bk, ldc)
    ctx.dma("sync", s_b[:, 16:24], bv, ldc)
    v_ldc = ldc.n
    for i, w in enumerate((WqT, WkT, WvT)):
        ctx.dma("gpsimd", r3(s_w[i][:, :], KC, D), fm_ap(w, D, 0, KC, D), wld)
    v_wld = wld.n
    ctx.wait("tensor", wld, v_wld)
    ctx.wait("vector", ldc, v_ldc)

    kev = [0, 0]
    kst = [[0] * 200 for _ in range(2)]
    gidx = 0
    NTI = 4
    for it in range(NTI):
        bi = it % 2
        if it >= 2:
            ctx.wait("sync", tc, kxuse[bi])
        v_lx = ctx.dma("sync", r3(s_x[bi][:, :], KC, TT), fm_ap(xn1b, 2048, it * TT, KC, TT), ldx[bi])
        ctx.wait("tensor", ldx[bi], v_lx)
        for wi, dst in ((0, o_q), (1, o_k), (2, o_v)):
            for oc in range(KC):
                pb = gidx % 2
                if gidx >= 2:
                    ctx.wait("tensor", vc, kev[pb])

                def mm(e, wi=wi, oc=oc, pb=pb, bi=bi):
                    for k in range(KC):
                        ins = e.matmul(ps[pb][:, :], s_w[wi][:, k * D + oc * 128: k * D + (oc + 1) * 128], s_x[bi][:, k * TT:(k + 1) * TT], start=(k == 0), stop=(k == KC - 1))
                    return ins
                v_mm = ctx.pe(mm, tc)
                ctx.wait("vector", tc, v_mm)
                ob = gidx % 2
                if gidx >= 2:
                    ctx.wait("vector", st[ob], kst[ob][gidx - 2])
                v_ev = ctx.vec(lambda e, wi=wi, oc=oc, pb=pb, ob=ob: e.tensor_scalar(s_o[ob][:, :], ps[pb][:, :], s_b[:, wi * 8 + oc:wi * 8 + oc + 1], None, ALU.add), vc)
                kev[pb] = v_ev
                ctx.wait("sync", vc, v_ev)
                if fused and wi > 0:
                    off = oc * 128 * 4096 + (wi - 1) * 2048 + it * TT
                    kst[ob][gidx] = ctx.dma("sync", bass.AP(dst.tensor, off, [[4096, 128], [1, TT]]), s_o[ob][:, :], st[ob])
                else:
                    kst[ob][gidx] = ctx.dma("sync", bass.AP(dst.tensor, oc * 128 * 2048 + it * TT, [[2048, 128], [1, TT]]), s_o[ob][:, :], st[ob])
                gidx += 1
        kxuse = kxuse if it > 0 else [0, 0]
        kxuse[bi] = tc.n

    for s in st:
        ctx.wait("sync", s, s.n)
    if fused:
        ctx.pop_scope()
    return ctx


OFFS = [0] + [2 ** kk for kk in range(12)]


def build_lattn(F=None):
    fused = F is not None
    if fused:
        ctx = F["ctx"]
        ctx.push_scope()
        _ph = "lattn"
        F["ext"].setdefault(_ph, {})

        def din(name, shape, dtype):
            if name in F["in"].get(_ph, {}):
                return F["in"][_ph][name]
            nm = _ph + "_" + name
            F["ext"][_ph][name] = nm
            return ctx.nc.dram_tensor(nm, shape, dtype, kind="ExternalInput").ap()

        def dout(name, shape, dtype):
            return F["out"][_ph][name]
    else:
        ctx = Ctx()
        din = ctx.din
        dout = ctx.dout
    q16 = din("q16i", [D, 2048], BF16)
    if fused:
        kvg = F["in"]["lattn"]["kvg"]
        kvo = F["in"]["lattn"]["kv_own"]
        masks = din("masks", [16, 2 * 13 * 1024], F32)
        x1f = din("x1i", [D, 2048], F32)
        g1f = din("g1i", [D, 2048], BF16)
        Kx = Vx = Vo = None
    else:
        Kx = din("Kx", [D, 4096], BF16)
        Vx = din("Vx", [D, 4096], BF16)
        Vo = din("Vo", [D, 2048], BF16)
    WoT = din("WoT", [D, D], F32)
    bo = din("bo", [128, 8], F32)
    hsel = din("hsel", [128, 8 * 16], BF16)
    hselT = din("hselT", [16, 8 * 128], F32)
    cvec = din("cvec", [128, 16], F32)
    o_s1 = dout("outh", [D, 2048], F32) if fused else dout("s1", [D, 2048], BF16)

    SW = 3072
    s_q = ctx.sbuf("s_q", [128, KC * 1024], BF16)
    s_K = ctx.sbuf("s_K", [128, KC * SW], BF16)
    s_V = ctx.sbuf("s_V", [128, KC * SW], BF16)
    s_acc = ctx.sbuf("s_acc", [128, KC * 1024], F32)
    s_M = [ctx.sbuf(f"s_M{i}", [128, 1024], BF16) for i in range(2)]
    s_e = ctx.sbuf("s_e", [16, 1024], F32)
    s_den = ctx.sbuf("s_den", [16, 1024], F32)
    s_rd = ctx.sbuf("s_rd", [16, 1024], F32)
    s_vs = ctx.sbuf("s_vs", [128, 8], F32)
    s_vt = ctx.sbuf("s_vt", [128, 1], F32)
    s_hsel = ctx.sbuf("s_hsel", [128, 8 * 16], BF16)
    s_hselT = ctx.sbuf("s_hselT", [16, 8 * 128], F32)
    s_mask = ctx.sbuf("s_mask", [16, 1024], F32) if fused else None
    s_at = ctx.sbuf("s_at", [128, KC * 1024], BF16)
    s_wo = ctx.sbuf("s_wo", [128, KC * D], BF16)
    s_bo = ctx.sbuf("s_bo", [128, 8], F32)
    s_t = ctx.sbuf("s_t7", [128, 1024], F32)
    if fused:
        s_x1t4 = [s_acc[:, i * 512:(i + 1) * 512] for i in range(3)]
        s_g1t4 = [s_acc[:, 1536 + i * 256:1536 + (i + 1) * 256].bitcast(BF16) for i in range(3)]
        s_o1 = [s_acc[:, 2304 + i * 512:2304 + (i + 1) * 512] for i in range(2)]
    else:
        s_o1 = [ctx.sbuf(f"s_o1{i}", [128, TT], BF16) for i in range(2)]
    s_cvec = ctx.sbuf("s_cvec7", [128, 16], F32)

    ps_s = ctx.psum("ps_s", [16, 1024])
    ps_er = [ctx.psum(f"ps_er{i}", [128, TT]) for i in range(2)]
    ps_o = [ctx.psum(f"ps_o{i}", [128, TT]) for i in range(2)]

    ldc = ctx.sem("ldc")
    wld = ctx.sem("wld")
    ldkv = ctx.sem("ldkv")
    ldsub = ctx.sem("ldsub")
    vc = ctx.sem("vc")
    sc = ctx.sem("sc")
    tc = ctx.sem("tc")
    st1 = [ctx.sem("st1_0"), ctx.sem("st1_1")]

    ctx.dma("sync", s_hsel[:, :], hsel, ldc)
    ctx.dma("sync", s_hselT[:, :], hselT, ldc)
    ctx.dma("sync", s_bo[:, :], bo, ldc)
    ctx.dma("sync", s_cvec[:, :], cvec, ldc)
    if fused:
        ldm = ctx.sem("ldm")
        kmused = [0]
    v_ldc = ldc.n
    ctx.dma("gpsimd", r3(s_wo[:, :], KC, D), fm_ap(WoT, D, 0, KC, D), wld)
    v_wld = wld.n
    ctx.wait("tensor", ldc, v_ldc)
    ctx.wait("tensor", wld, v_wld)
    ctx.wait("vector", ldc, v_ldc)
    ctx.wait("scalar", ldc, v_ldc)

    v_z = ctx.vec(lambda e: e.memset(s_vs[:, :], 0.0), vc)
    if fused:
        vsrc = [(kvg, 4096, 4 + j, r) for r in range(2) for j in range(4)]
    else:
        vsrc = [(Vo, 2048, j, 0) for j in range(4)] + [(Vx, 4096, j, 0) for j in range(8)]
    kvsum = 0
    for (src_t, ncols, j, r0) in vsrc:
        if kvsum:
            ctx.wait("sync", vc, kvsum)
        v_l = ctx.dma("sync", r3(s_at[:, 0:KC * TT], KC, TT), fm_ap(src_t, ncols, j * TT, KC, TT, row0=r0 * 1024), ldkv)
        ctx.wait("vector", ldkv, v_l)
        for ch in range(KC):
            ctx.vec(lambda e, ch=ch: e.reduce_sum(s_vt[:, :], s_at.bitcast(BF16)[:, ch * TT:(ch + 1) * TT] if False else s_at[:, ch * TT:(ch + 1) * TT], axis=AX), vc)
            ctx.vedge("vector", vc)
            ctx.vec(lambda e, ch=ch: e.tensor_tensor(s_vs[:, ch:ch + 1], s_vs[:, ch:ch + 1], s_vt[:, :], ALU.add), vc)
            ctx.vedge("vector", vc)
        kvsum = vc.n

    gst = 0
    kst = [[0] * 40 for _ in range(2)]
    ker = [0, 0]
    koev = [0, 0]
    if fused:
        ldxg3 = [ctx.sem(f"ldxg{i}") for i in range(3)]
    kcmb = [0] * 3
    vldt = [0] * 16
    kst1_last = [0, 0]
    for sub in range(2):
        if sub >= 1:
            ctx.wait("sync", vc, vc.n)
            ctx.wait("sync", tc, tc.n)
        v_lq = ctx.dma("sync", r3(s_q[:, :], KC, 1024), fm_ap(q16, 2048, sub * 1024, KC, 1024), ldsub)
        if fused:
            wctx = 2048 - sub * 1024
            wown = 3072 - wctx
            ctx.dma("sync", s_K[:, :].rearrange("p (k t) -> p k t", k=KC)[:, :, 0:wctx],
                    bass.AP(kvg.tensor, sub * 1024, [[4096, 128], [4096 * 128, KC], [1, wctx]]), ldsub)
            ctx.dma("sync", s_K[:, :].rearrange("p (k t) -> p k t", k=KC)[:, :, wctx:SW],
                    bass.AP(kvo.tensor, 0, [[4096, 128], [4096 * 128, KC], [1, wown]]), ldsub)
            ctx.dma("sync", s_V[:, :].rearrange("p (k t) -> p k t", k=KC)[:, :, 0:wctx],
                    bass.AP(kvg.tensor, 2048 + sub * 1024, [[4096, 128], [4096 * 128, KC], [1, wctx]]), ldsub)
            v_lV = ctx.dma("sync", s_V[:, :].rearrange("p (k t) -> p k t", k=KC)[:, :, wctx:SW],
                           bass.AP(kvo.tensor, 2048, [[4096, 128], [4096 * 128, KC], [1, wown]]), ldsub)
        else:
            ctx.dma("sync", r3(s_K[:, :], KC, SW), fm_ap(Kx, 4096, sub * 1024, KC, SW), ldsub)
            v_lV = ctx.dma("sync", r3(s_V[:, :], KC, SW), fm_ap(Vx, 4096, sub * 1024, KC, SW), ldsub)
        ctx.wait("vector", ldsub, v_lV)

        if fused and sub >= 1:
            ctx.wait("vector", st1[0], kst1_last[0])
            ctx.wait("vector", st1[1], kst1_last[1])
        ctx.vec(lambda e: e.memset(s_acc[:, :], 0.0), vc)
        v_zd = ctx.vec(lambda e: e.memset(s_den[:, :], 0.0), vc)

        for io, o in enumerate(OFFS):
            for ch in range(KC):
                mb = ch % 2
                gm = (sub * len(OFFS) + io) * KC + ch
                if gm >= 2:
                    ctx.wait("vector", tc, kssm[mb])
                v_M = ctx.vec(lambda e, ch=ch, o=o, mb=mb: e.tensor_tensor(s_M[mb][:, :], s_q[:, ch * 1024:(ch + 1) * 1024], s_K[:, ch * SW + 2048 - o: ch * SW + 3072 - o], ALU.mult), vc)
                ctx.wait("tensor", vc, v_M)
                if io > 0 and ch == 0:
                    ctx.wait("tensor", sc, kexp)

                def mm_s(e, ch=ch, mb=mb):
                    e.matmul(ps_s[:, 0:512], s_hsel[:, ch * 16:(ch + 1) * 16], s_M[mb][:, 0:512], start=(ch == 0), stop=(ch == KC - 1))
                    return e.matmul(ps_s[:, 512:1024], s_hsel[:, ch * 16:(ch + 1) * 16], s_M[mb][:, 512:1024], start=(ch == 0), stop=(ch == KC - 1))
                v_ss = ctx.pe(mm_s, tc)
                if gm == 0:
                    kssm = [0, 0]
                kssm[mb] = v_ss
            ctx.wait("scalar", tc, v_ss)
            ctx.wait("scalar", vc, vc.n)
            kexp = ctx.act(s_e[:, :], ps_s[:, :], AF.Exp, sem=sc, scale=s_cvec[0:16, 4:5])
            if fused:
                moff = (sub * 13 + io) * 1024
                ctx.wait("sync", vc, kmused[0])
                v_lm = ctx.dma("sync", s_mask[:, :], bass.AP(masks.tensor, moff, [[2 * 13 * 1024, 16], [1, 1024]]), ldm)
            ctx.wait("vector", sc, kexp)
            ctx.vec(lambda e: e.tensor_scalar(s_e[:, :], s_e[:, :], -1.0, None, ALU.add), vc)
            ctx.vedge("vector", vc)
            if fused:
                ctx.wait("vector", ldm, v_lm)
                kmused[0] = ctx.vec(lambda e: e.tensor_tensor(s_e[:, :], s_e[:, :], s_mask[:, :], ALU.mult), vc)
                ctx.vedge("vector", vc)
            v_den = ctx.vec(lambda e: e.tensor_tensor(s_den[:, :], s_den[:, :], s_e[:, :], ALU.add), vc)
            ctx.wait("tensor", vc, v_den)
            for ch in range(KC):
                pb = ch % 2
                for half in range(2):
                    ctx.wait("tensor", vc, ker[pb])

                    def mm_h(e, ch=ch, pb=pb, half=half):
                        return e.matmul(ps_er[pb][:, :], s_hselT[:, ch * 128:(ch + 1) * 128], s_e[:, half * 512:(half + 1) * 512], start=True, stop=True)
                    v_er = ctx.pe(mm_h, tc)
                    ctx.wait("vector", tc, v_er)
                    v_t = ctx.vec(lambda e, ch=ch, o=o, pb=pb, half=half: e.tensor_tensor(s_t[:, half * 512:(half + 1) * 512], ps_er[pb][:, :], s_V[:, ch * SW + 2048 - o + half * 512: ch * SW + 2048 - o + (half + 1) * 512], ALU.mult), vc)
                    ker[pb] = v_t
                    ctx.vedge("vector", vc)
                    ctx.vec(lambda e, ch=ch, half=half: e.tensor_tensor(s_acc[:, ch * 1024 + half * 512: ch * 1024 + (half + 1) * 512], s_acc[:, ch * 1024 + half * 512: ch * 1024 + (half + 1) * 512], s_t[:, half * 512:(half + 1) * 512], ALU.add), vc)
                    ctx.vedge("vector", vc)

        ctx.vec(lambda e: e.tensor_scalar(s_rd[:, :], s_den[:, :], s_cvec[0:16, 6:7], None, ALU.add), vc)
        ctx.vedge("vector", vc)
        v_rd = ctx.vec(lambda e: e.reciprocal(s_rd[:, :], s_rd[:, :]), vc)
        for ch in range(KC):
            pb = ch % 2
            for half in range(2):
                ctx.wait("tensor", vc, v_rd)
                ctx.wait("tensor", vc, ker[pb])

                def mm_rd(e, ch=ch, pb=pb, half=half):
                    return e.matmul(ps_er[pb][:, :], s_hselT[:, ch * 128:(ch + 1) * 128], s_rd[:, half * 512:(half + 1) * 512], start=True, stop=True)
                v_r = ctx.pe(mm_rd, tc)
                ctx.wait("vector", tc, v_r)
                ctx.vec(lambda e, ch=ch, half=half: e.tensor_scalar(s_t[:, half * 512:(half + 1) * 512], s_acc[:, ch * 1024 + half * 512: ch * 1024 + (half + 1) * 512], s_vs[:, ch:ch + 1], None, ALU.add), vc)
                ctx.vedge("vector", vc)
                v_at = ctx.vec(lambda e, ch=ch, pb=pb, half=half: e.tensor_tensor(s_at[:, ch * 1024 + half * 512: ch * 1024 + (half + 1) * 512], s_t[:, half * 512:(half + 1) * 512], ps_er[pb][:, :], ALU.mult), vc)
                ker[pb] = v_at
        v_atdone = vc.n

        ctx.wait("tensor", vc, v_atdone)

        def pref(j2):
            tt2p, ocp = j2 // KC, j2 % KC
            if kcmb[j2 % 3]:
                ctx.wait("sync", vc, kcmb[j2 % 3])
            ctx.dma("sync", s_x1t4[j2 % 3][:, :], bass.AP(x1f.tensor, ocp * 128 * 2048 + sub * 1024 + tt2p * TT, [[2048, 128], [1, TT]]), ldxg3[j2 % 3])
            ctx.dma("sync", s_g1t4[j2 % 3][:, :], bass.AP(g1f.tensor, ocp * 128 * 2048 + sub * 1024 + tt2p * TT, [[2048, 128], [1, TT]]), ldxg3[j2 % 3])
            vldt[j2] = ldxg3[j2 % 3].n

        if fused:
            ctx.wait("sync", vc, v_atdone)
            for j in range(2):
                pref(j)
        for tt2 in range(2):
            for oc in range(KC):
                j = tt2 * KC + oc
                pb = j % 2
                if gst >= 2:
                    ctx.wait("tensor", vc, koev[pb])
                if fused and j + 2 < 16:
                    pref(j + 2)

                def mm_o(e, oc=oc, pb=pb, tt2=tt2):
                    for k in range(KC):
                        ins = e.matmul(ps_o[pb][:, :], s_wo[:, k * D + oc * 128: k * D + (oc + 1) * 128], s_at[:, k * 1024 + tt2 * TT: k * 1024 + (tt2 + 1) * TT], start=(k == 0), stop=(k == KC - 1))
                    return ins
                v_mo = ctx.pe(mm_o, tc)
                ctx.wait("vector", tc, v_mo)
                ob = gst % 2
                if gst >= 2:
                    ctx.wait("vector", st1[ob], kst[ob][gst - 2])
                v_oe = ctx.vec(lambda e, oc=oc, pb=pb, ob=ob: e.tensor_scalar(s_o1[ob][:, :], ps_o[pb][:, :], s_bo[:, oc:oc + 1], None, ALU.add), vc)
                koev[pb] = v_oe
                if fused:
                    sl = j % 3
                    ctx.wait("vector", ldxg3[sl], vldt[j])
                    ctx.vedge("vector", vc)
                    ctx.vec(lambda e, ob=ob, sl=sl: e.tensor_tensor(s_o1[ob][:, :], s_o1[ob][:, :], s_g1t4[sl][:, :], ALU.mult))
                    ctx.vedge("vector", vc)
                    v_oe = ctx.vec(lambda e, ob=ob, sl=sl: e.tensor_tensor(s_o1[ob][:, :], s_o1[ob][:, :], s_x1t4[sl][:, :], ALU.add), vc)
                    kcmb[sl] = v_oe
                    koev[pb] = v_oe
                ctx.wait("sync", vc, v_oe)
                kst[ob][gst] = ctx.dma("sync", bass.AP(o_s1.tensor, oc * 128 * 2048 + sub * 1024 + tt2 * TT, [[2048, 128], [1, TT]]), s_o1[ob][:, :], st1[ob])
                gst += 1
        kst1_last = [st1[0].n, st1[1].n]
        if sub == 0:
            pass
        if sub == 1:
            pass

    for s in st1:
        ctx.wait("sync", s, s.n)
    if fused:
        ctx.pop_scope()
    return ctx


def build_lffn(F=None):
    fused = F is not None
    if fused:
        ctx = F["ctx"]
        ctx.push_scope()
        _ph = "lffn"
        F["ext"].setdefault(_ph, {})

        def din(name, shape, dtype):
            if name in F["in"].get(_ph, {}):
                return F["in"][_ph][name]
            nm = _ph + "_" + name
            F["ext"][_ph][name] = nm
            return ctx.nc.dram_tensor(nm, shape, dtype, kind="ExternalInput").ap()

        def dout(name, shape, dtype):
            return F["out"][_ph][name]
    else:
        ctx = Ctx()
        din = ctx.din
        dout = ctx.dout
    xn1 = din("xn1i", [D, 2048], F32)
    qws = [din(f"qw{i}", [D, D], I8) for i in range(6)]
    scs = [din(f"sc{i}", [8, 8 * D], F32) for i in range(6)]
    bis = [din(f"bi{i}", [128, 8], F32) for i in range(6)]
    las = [din(f"la{i}", [D, 32], F32) for i in range(6)]
    lbs = [din(f"lb{i}", [32, D], F32) for i in range(6)]
    lgs = [din(f"lg{i}", [128, 8], F32) for i in range(3)]
    lbt = [din(f"lbt{i}", [128, 8], F32) for i in range(3)]
    g8 = din("g8", [8, 128], F32)
    ones1f = din("ones1f", [1, 128], F32)
    ones128b = din("ones128b", [128, 1], BF16)
    cvec = din("cvec", [128, 16], F32)
    o_out = dout("g1h", [D, 2048], BF16)

    s_W = [ctx.sbuf(f"s_W{i}", [128, KC * D], BF16) for i in range(2)]
    s_hq = ctx.sbuf("s_hq", [128, KC * 2048], F32)
    s_hb = ctx.sbuf("s_hb", [128, KC * TT], BF16)
    s_y = ctx.sbuf("s_y", [128, KC * TT], F32)
    s_yg = ctx.sbuf("s_yg", [128, KC * TT], BF16)
    s_t1 = ctx.sbuf("s_t1", [32, TT], BF16)
    s_la = [ctx.sbuf(f"s_la{i}", [128, KC * 32], BF16) for i in range(6)]
    s_lb = [ctx.sbuf(f"s_lb{i}", [32, D], BF16) for i in range(6)]
    s_bi = ctx.sbuf("s_bi", [128, 8 * 6], F32)
    s_lg = ctx.sbuf("s_lg", [128, 8 * 3], F32)
    s_lbt = ctx.sbuf("s_lbt", [128, 8 * 3], F32)
    s_g8 = ctx.sbuf("s_g8", [8, 128], F32)
    s_ws = [ctx.sbuf("s_ws0", [128, D], F32)] * 2
    s_sct = [ctx.sbuf("s_sct0", [8, D], F32)] * 2

    s_ones1f = ctx.sbuf("s_ones1f", [1, 128], F32)
    s_ones128b = ctx.sbuf("s_ones128b", [128, 1], BF16)
    s_cvec = ctx.sbuf("s_cvec", [128, 16], F32)

    kit = LNKit(ctx, TT, "l8")
    ps_mm = [ctx.psum(f"ps_mm{i}", [128, TT]) for i in range(2)]
    ps_l = ctx.psum("ps_l", [32, TT])

    ldc = ctx.sem("ldc")
    wld = ctx.sem("wld")
    ldt = ctx.sem("ldt")
    ldh = ctx.sem("ldh")
    ldw = [ctx.sem("ldw0"), ctx.sem("ldw1")]
    lds = [ctx.sem("lds0"), ctx.sem("lds1")]
    vc = ctx.sem("vc")
    sc = ctx.sem("sc")
    tc = ctx.sem("tc")
    st_o = ctx.sem("st_o")

    for t, d in ((s_g8, g8), (s_ones1f, ones1f), (s_ones128b, ones128b), (s_cvec, cvec)):
        ctx.dma("sync", t[:, :], d, ldc)
    for i in range(6):
        ctx.dma("sync", s_bi[:, 8 * i:8 * i + 8], bis[i], ldc)
    for i in range(3):
        ctx.dma("sync", s_lg[:, 8 * i:8 * i + 8], lgs[i], ldc)
        ctx.dma("sync", s_lbt[:, 8 * i:8 * i + 8], lbt[i], ldc)
    v_ldc = ldc.n
    for i in range(6):
        ctx.dma("gpsimd", r3(s_la[i][:, :], KC, 32), fm_ap(las[i], 32, 0, KC, 32), wld)
        ctx.dma("gpsimd", s_lb[i][:, :], lbs[i], wld)
    v_wld = wld.n
    ctx.wait("vector", ldc, v_ldc)
    ctx.wait("scalar", ldc, v_ldc)
    ctx.wait("tensor", ldc, v_ldc)
    ctx.wait("tensor", wld, v_wld)

    v_lh = ctx.dma("sync", r3(s_hq[:, :], KC, 2048), fm_ap(xn1, 2048, 0, KC, 2048), ldh)

    env = {"vc": vc, "sc": sc, "tc": tc, "ones1f": s_ones1f, "ones128b": s_ones128b}
    NTI = 4
    k_wuse = [0, 0]
    k_deq = [0, 0]
    k_hb = 0
    k_t1 = 0
    kev = [0, 0]
    k_lps = 0

    def dequant(L, slot):
        nonlocal_vals = None
        for kc in range(KC):
            wb = 0
            ctx.wait("gpsimd", vc, va_prev[0])
            ctx.wait("gpsimd", vc, max(kdq))
            v_w = ctx.dma("gpsimd", s_ws[wb][:, :], bass.AP(qws[L].tensor, kc * 128 * D, [[D, 128], [1, D]]), ldw[wb])
            v_s = ctx.dma("gpsimd", s_sct[wb][:, :], bass.AP(scs[L].tensor, kc * D, [[8 * D, 8], [1, D]]), lds[wb])
            ctx.wait("vector", ldw[wb], v_w)
            v_a = ctx.vec(lambda e, wb=wb: e.tensor_scalar(s_ws[wb][:, :], s_ws[wb][:, :], s_cvec[:, 8:9], -1.0, ALU.mult, ALU.add), vc)
            va_prev[0] = v_a
            ctx.wait("tensor", lds[wb], v_s)
            for half in range(2):
                pb = half
                ctx.wait("tensor", vc, kdq[pb])
                ctx.wait("tensor", vc, kev[pb])

                def mm_sr(e, wb=wb, half=half, pb=pb):
                    return e.matmul(ps_mm[pb][:, :], s_g8[:, :], s_sct[wb][:, half * TT:(half + 1) * TT], start=True, stop=True)
                v_sr = ctx.pe(mm_sr, tc)
                ctx.wait("vector", tc, v_sr)
                ctx.vedge("vector", vc)
                kdq[pb] = ctx.vec(lambda e, L=L, slot=slot, kc=kc, wb=wb, half=half, pb=pb: e.tensor_tensor(
                    s_W[slot][:, kc * D + half * TT: kc * D + (half + 1) * TT], s_ws[wb][:, half * TT:(half + 1) * TT], ps_mm[pb][:, :], ALU.mult), vc)
        return vc.n

    kdq = [0, 0]
    va_prev = [0]
    for blk in range(3):
        La, Lb = 2 * blk, 2 * blk + 1
        ctx.wait("vector", tc, k_wuse[0])
        ctx.wait("vector", tc, k_wuse[1])
        v_dqa = dequant(La, 0)
        v_dqb = dequant(Lb, 1)
        k_deq = [v_dqa, v_dqb]
        if blk == 0:
            ctx.wait("vector", ldh, v_lh)

        for it in range(NTI):
            ctx.vedge("vector", vc)
            if k_hb:
                ctx.wait("vector", tc, k_hb)
            for k in range(KC):
                last = (k == KC - 1)
                ctx.vec(lambda e, k=k, it=it: e.tensor_copy(s_hb[:, k * TT:(k + 1) * TT], s_hq[:, k * 2048 + it * TT: k * 2048 + (it + 1) * TT]), vc if last else None)
            v_hb = vc.n

            for j, (L, slot) in enumerate(((La, 0), (Lb, 1))):
                src16 = s_hb if j == 0 else s_yg
                v_src = v_hb if j == 0 else v_yg
                ctx.wait("tensor", vc, v_src)
                ctx.wait("tensor", vc, k_deq[slot])
                ctx.wait("tensor", vc, k_t1)

                def mm_l(e, L=L, src16=src16):
                    for k in range(KC):
                        ins = e.matmul(ps_l[:, :], s_la[L][:, k * 32:(k + 1) * 32], src16[:, k * TT:(k + 1) * TT], start=(k == 0), stop=(k == KC - 1))
                    return ins
                v_ml = ctx.pe(mm_l, tc)
                ctx.wait("vector", tc, v_ml)
                k_t1 = ctx.vec(lambda e: e.tensor_copy(s_t1[:, :], ps_l[:, :]), vc)

                for oc in range(KC):
                    pb = oc % 2
                    ctx.wait("tensor", vc, kev[pb])
                    ctx.wait("tensor", vc, k_t1)

                    def mm_b(e, L=L, slot=slot, oc=oc, pb=pb, src16=src16):
                        for k in range(KC):
                            e.matmul(ps_mm[pb][:, :], s_W[slot][:, k * D + oc * 128: k * D + (oc + 1) * 128], src16[:, k * TT:(k + 1) * TT], start=(k == 0), stop=False)
                        return e.matmul(ps_mm[pb][:, :], s_lb[L][:, oc * 128:(oc + 1) * 128], s_t1[:, :], start=False, stop=True)
                    v_mb = ctx.pe(mm_b, tc)
                    k_wuse[slot] = v_mb
                    ctx.wait("vector", tc, v_mb)
                    if j == 0:
                        kev[pb] = ctx.vec(lambda e, L=L, oc=oc, pb=pb: e.tensor_scalar(s_y[:, oc * TT:(oc + 1) * TT], ps_mm[pb][:, :], s_bi[:, 8 * L + oc:8 * L + oc + 1], None, ALU.add), vc)
                    else:
                        kev[pb] = ctx.vec(lambda e, L=L, oc=oc, pb=pb, it=it: e.scalar_tensor_tensor(
                            s_hq[:, oc * 2048 + it * TT: oc * 2048 + (it + 1) * TT], ps_mm[pb][:, :], s_bi[:, 8 * L + oc:8 * L + oc + 1],
                            s_hq[:, oc * 2048 + it * TT: oc * 2048 + (it + 1) * TT], ALU.add, ALU.add), vc)

                if j == 0:
                    def out_writes(k):
                        ctx.vec(lambda e, k=k: e.tensor_tensor(s_y[:, k * TT:(k + 1) * TT], s_y[:, k * TT:(k + 1) * TT], kit.ps_b1[:, :], ALU.subtract), vc)
                        ctx.vedge("vector", vc)
                        v_n = ctx.vec(lambda e, k=k: e.tensor_tensor(s_y[:, k * TT:(k + 1) * TT], s_y[:, k * TT:(k + 1) * TT], kit.ps_b2[:, :], ALU.mult), vc)
                        ctx.wait("scalar", vc, v_n)
                        ctx.act(s_yg[:, k * TT:(k + 1) * TT], s_y[:, k * TT:(k + 1) * TT], AF.Gelu, sem=sc,
                                scale=s_lg[:, 8 * blk + k:8 * blk + k + 1], bias=s_lbt[:, 8 * blk + k:8 * blk + k + 1])
                        return v_n

                    ctx.vedge("vector", vc)
                    kit.emit(env, lambda k: s_y[:, k * TT:(k + 1) * TT], out_writes, s_cvec[0:1, 0:1])
                    v_yg = sc.n
                    ctx.wait("tensor", sc, v_yg)

    k_ost = 0
    ctx.wait("scalar", vc, vc.n)
    for it in range(NTI):
        if it >= 1:
            ctx.wait("scalar", st_o, k_ost)
        for k in range(KC):
            v_sil = ctx.act(s_hb[:, k * TT:(k + 1) * TT], s_hq[:, k * 2048 + it * TT: k * 2048 + (it + 1) * TT], AF.Silu, sem=sc)
        ctx.wait("sync", sc, sc.n)
        k_ost = ctx.dma("sync", fm_ap(o_out, 2048, it * TT, KC, TT), r3(s_hb[:, :], KC, TT), st_o)

    ctx.wait("sync", st_o, st_o.n)
    if fused:
        ctx.pop_scope()
    return ctx


def host_prep_lffn(inp):
    f32 = np.float32
    P = {}
    P["qw"] = [Tc(np.asarray(inp["L1_ffn_qweights"][i], np.int8)) for i in range(6)]
    P["sc"] = [np.ascontiguousarray(Tc(np.asarray(inp["L1_ffn_scales"][i], f32)).reshape(8, 8, D).transpose(1, 0, 2).reshape(8, 8 * D)) for i in range(6)]
    P["bi"] = [pmaj(np.asarray(inp["L1_ffn_biases"][i], f32)) for i in range(6)]
    P["la"] = [Tc(np.asarray(inp["L1_ffn_lora_a"][i], f32)) for i in range(6)]
    P["lb"] = [Tc(np.asarray(inp["L1_ffn_lora_b"][i], f32)) for i in range(6)]
    P["lg"] = [pmaj(np.asarray(inp["L1_ffn_ln_gamma"][i], f32)) for i in range(3)]
    P["lbt"] = [pmaj(np.asarray(inp["L1_ffn_ln_beta"][i], f32)) for i in range(3)]
    g8 = np.zeros((8, 128), f32)
    for p in range(128):
        g8[p // 16, p] = 1
    P["g8"] = g8
    return P


def host_prep_pre(inp):
    f32 = np.float32
    P = {}
    x = np.asarray(inp["L0_attn_input_tensor"], f32)
    P["x0T"] = [Tc(x[b]) for b in range(B)]
    ipT_full = Tc(np.asarray(inp["L0_attn_in_proj_w"], f32))
    P["ipT"] = [np.ascontiguousarray(np.concatenate(
        [ipT_full[:, h * DHALF:(h + 1) * DHALF],
         ipT_full[:, DI + h * DHALF: DI + (h + 1) * DHALF]], axis=1)) for h in range(2)]
    cw = np.asarray(inp["L0_attn_conv_w"], f32)
    P["convw"] = [pmaj(cw[h * DHALF:(h + 1) * DHALF, :]) for h in range(2)]
    P["convb"] = [pmaj(np.asarray(inp["L0_attn_conv_b"], f32)[h * DHALF:(h + 1) * DHALF]) for h in range(2)]
    xpT = Tc(np.asarray(inp["L0_attn_x_proj_w"], f32))
    xpp = np.zeros((DI, 128), f32)
    xpp[:, 0:16] = xpT[:, 64:80]
    xpp[:, 32:48] = xpT[:, 80:96]
    xpp[:, 64:128] = xpT[:, 0:64]
    P["xppT"] = [np.ascontiguousarray(xpp[h * DHALF:(h + 1) * DHALF]) for h in range(2)]
    P["ones1f"] = np.ones((1, 128), f32)
    P["ones128b"] = np.ones((128, 1), NB16)
    CV = np.zeros((128, 16), f32)
    CV[:, 0] = 1e-5
    CV[:, 1] = 1e-12
    CV[:, 2] = 1.0 / 127.0
    CV[:, 3] = 1.0 / 16129.0
    CV[:, 4] = 0.125
    CV[:, 5] = 1.0 / 1024.0
    CV[:, 6] = 4096.0
    CV[:, 7] = 1.0 / 32.0
    CV[:, 8] = 1.0 / 7.5
    P["cvec"] = CV
    return P


def maps_pre(P):
    maps = []
    for c in range(8):
        b, h = c // 2, c % 2
        maps.append({
            "x0T": P["x0T"][b], "ipT": P["ipT"][h], "convw": P["convw"][h],
            "convb": P["convb"][h], "xppT": P["xppT"][h],
            "ones1f": P["ones1f"], "ones128b": P["ones128b"], "cvec": P["cvec"],
        })
    return maps


DEBUG_DUMP = False


def build_fused():
    ctx = Ctx()
    nc = ctx.nc
    hofs = nc.dram_tensor("hofs", [1, 1], I32, kind="ExternalInput").ap()
    s_hofs = ctx.sbuf("s_hofs", [1, 1], I32)
    rb = nc.sync.alloc_register("rb_h")
    ldh = ctx.sem("ldh0")
    ctx.dma("sync", s_hofs[:, :], hofs, ldh)
    ctx.wait("sync", ldh, 16)
    ctx.op("sync", lambda e: e.reg_load(rb, s_hofs[0:1, 0:1]))
    _dynregs = []

    def dyn_dma(c2, dst, tensor_ap, const, pattern, sem):
        v = sem.inc(16)
        rt = nc.sync.alloc_register(f"rt_h{len(_dynregs)}")
        _dynregs.append(rt)

        def f(e, dst=dst, tensor_ap=tensor_ap, const=const, pattern=pattern, sem=sem, rt=rt):
            e.reg_add(rt, rb, const)
            e.dma_start(dst, bass.AP(tensor_ap.tensor, rt, pattern)).then_inc(sem.h, 16)
        c2.ops["sync"].append(f)
        return v

    xn0 = ctx.dram("i_xn0", [D, L], F32)
    xn16 = ctx.dram("i_xn16", [D, L], BF16)
    u16 = ctx.dram("i_u16", [DHALF, L], BF16)
    res16 = ctx.dram("i_res16", [DHALF, L], BF16)
    xdblp = ctx.dram("i_xdblp", [128, L], F32)
    xdbls = ctx.dram("i_xdbls", [128, L], F32)
    mps = ctx.dram("i_mps", [2048, 2048], F32)
    m_own = ctx.dram("i_m", [D, 2048], F32)
    ym = ctx.dram("i_ym", [DHALF, L], BF16)
    g0 = ctx.dram("i_g0", [D, 2048], BF16)
    h0i = ctx.dram("i_h0", [D, 2048], F32)
    k16 = ctx.dram("i_k16", [D, 2048], BF16)
    statsp = ctx.dram("i_statsp", [128, 9], F32)
    statss = ctx.dram("i_statss", [128, 9], F32)
    h2 = ctx.dram("i_h2", [D, 2048], F32)
    h2b = ctx.dram("i_h2b", [D, 2048], BF16)
    x1 = ctx.dram("i_x1", [D, 2048], F32)
    xn1 = ctx.dram("i_xn1", [D, 2048], F32)
    xn1b = ctx.dram("i_xn1b", [D, 2048], BF16)
    q1 = ctx.dram("i_q1", [D, 2048], BF16)
    kv_own = ctx.dram("i_kvown", [D, 4096], BF16)
    kvg = ctx.dram("i_kvg", [2048, 4096], BF16)
    g1l = ctx.dram("i_g1", [D, 2048], BF16)
    o_out = ctx.dout("outh", [D, 2048], F32)

    F = {
        "ctx": ctx, "ext": {}, "dyn_dma": dyn_dma,
        "in": {
            "scan": {"xdbl0": xdbls, "u16i": u16, "res16i": res16},
            "ffn0": {} if FFN0_DBG else {"xn16i": xn16},
            "attn1": {"mp0": m_own, "xn0h": xn0},
            "attn2": {"st0": statss, "h0i": h0i, "k16i": k16},
            "comb": {"h2i": h2, "h2bi": h2b, "g0i": g0},
            "qkv": {"xn1bi": xn1b},
            "lattn": {"q16i": q1, "kvg": kvg, "kv_own": kv_own, "x1i": x1, "g1i": g1l},
            "lffn": {"xn1i": xn1},
        },
        "out": {
            "pre": {"xn0": xn0, "xn16": xn16, "u16": u16, "res16": res16, "xdblp": xdblp},
            "scan": {"mp": mps, "ym": ym},
            "ffn0": {"g0": g0},
            "attn1": {"h0": h0i, "k16": k16, "stats": statsp},
            "attn2": {"h2": h2, "h2b": h2b},
            "comb": {"x1": x1, "xn1": xn1, "xn1b": xn1b},
            "qkv": {"q1": q1, "kv": kv_own},
            "lattn": {"outh": o_out},
            "lffn": {"g1h": g1l},
        },
    }

    csem = ctx.sem("coll")

    def wait_all(v):
        for eng in Ctx.ENGINES:
            ctx.wait(eng, csem, v)

    build_pre(F)
    ctx.join()
    v = ctx.collective("AllReduce", ALU.add, xdblp, xdbls, csem)
    wait_all(v)
    build_scan(F)
    ctx.join()
    v = ctx.collective("ReduceScatter", ALU.add, mps, m_own, csem)
    wait_all(v)
    build_ffn0(F)
    ctx.join()
    build_attn1(F)
    ctx.join()
    v = ctx.collective("AllReduce", ALU.add, statsp, statss, csem)
    wait_all(v)
    build_attn2(F)
    ctx.join()
    build_comb(F)
    ctx.join()
    build_qkv(F)
    ctx.join()
    v = ctx.collective("AllGather", ALU.bypass, kv_own, kvg, csem)
    build_lffn(F)
    ctx.join()
    wait_all(v)
    build_lattn(F)
    if DEBUG_DUMP:
        ctx.join()
        dsem = ctx.sem("dbg")
        for nm, t in (("d_xn0", xn0), ("d_xn16", xn16), ("d_xdbls", xdbls), ("d_m", m_own), ("d_g0", g0),
                      ("d_h0", h0i), ("d_statss", statss), ("d_h2", h2), ("d_x1", x1),
                      ("d_xn1", xn1), ("d_q1", q1), ("d_kvown", kv_own), ("d_kvg", kvg),
                      ("d_g1", g1l), ("d_u16", u16), ("d_ym", ym)):
            o = ctx.dout(nm, list(t.shape), t.dtype)
            ctx.dma("sync", o, t, dsem)
        ctx.wait("sync", dsem, dsem.n)
    return ctx, F


def fused_in_maps(inputs):
    f32 = np.float32
    P = host_prep_pre(inputs)
    Ps = host_prep_scan(inputs)
    Pf = host_prep_ffn0(inputs)
    Pa1 = host_prep_attn1(inputs)
    Pa2 = host_prep_attn2(inputs)
    Pl = host_prep_lffn(inputs)
    WqT = Tc(np.asarray(inputs["L1_attn_Wq"], f32))
    WkT = Tc(np.asarray(inputs["L1_attn_Wk"], f32))
    WvT = Tc(np.asarray(inputs["L1_attn_Wv"], f32))
    WoT = Tc(np.asarray(inputs["L1_attn_Wo"], f32))
    hsel = np.zeros((128, 8 * 16), NB16)
    hselT = np.zeros((16, 8 * 128), f32)
    for ch in range(8):
        for p in range(128):
            j = 2 * ch + p // 64
            hsel[p, ch * 16 + j] = 1
            hselT[j, ch * 128 + p] = 1
    maps = []
    for c in range(8):
        b, h = c // 2, c % 2
        sl = slice(h * 2048, (h + 1) * 2048)
        mask = np.zeros((16, 2 * 13 * 1024), f32)
        for sub in range(2):
            for io, o in enumerate(OFFS):
                base = h * 2048 + sub * 1024
                valid = (np.arange(1024) + base) >= o
                mask[:, (sub * 13 + io) * 1024:(sub * 13 + io + 1) * 1024] = valid[None, :].astype(f32)
        m = {
            "hofs": np.array([[h * 2048]], np.int32),
            "pre_x0T": P["x0T"][b], "pre_ipT": P["ipT"][h], "pre_convw": P["convw"][h],
            "pre_convb": P["convb"][h], "pre_xppT": P["xppT"][h],
            "pre_ones1f": P["ones1f"], "pre_ones128b": P["ones128b"], "pre_cvec": P["cvec"],
            "scan_dtpT": Ps["dtpT"][h], "scan_dtb": Ps["dtb"][h], "scan_AlogP": Ps["AlogP"][h],
            "scan_DskP": Ps["DskP"][h], "scan_opT": Ps["opT"][h], "scan_sel16": Ps["sel16"],
            "scan_ident": Ps["ident"],
            "ffn0_wgT": Pf["wgT"], "ffn0_wuT": Pf["wuT"], "ffn0_wdT": Pf["wdT"], "ffn0_cvec": P["cvec"],
            "attn1_qwT": Pa1["qwT"], "attn1_kwT": Pa1["kwT"], "attn1_qb": Pa1["qb"],
            "attn1_kb": Pa1["kb"], "attn1_wg": Pa1["wg"], "attn1_ln0g": Pa1["ln0g"],
            "attn1_ln0b": Pa1["ln0b"], "attn1_ones1f": P["ones1f"], "attn1_ones128b": P["ones128b"],
            "attn1_cvec": P["cvec"],
            "attn2_projT": Pa2["projT"], "attn2_flT": Pa2["flT"], "attn2_pb": Pa2["pb"],
            "attn2_fb": Pa2["fb"], "attn2_lnig": Pa2["lnig"], "attn2_lnib": Pa2["lnib"],
            "attn2_ln1g": Pa2["ln1g"], "attn2_ln1b": Pa2["ln1b"], "attn2_ones1f": P["ones1f"],
            "attn2_ones128b": P["ones128b"], "attn2_cvec": P["cvec"],
            "comb_x0h": np.ascontiguousarray(P["x0T"][b][:, sl]),
            "comb_d1T": Pa2["d1T"], "comb_d2T": Pa2["d2T"], "comb_d1b": Pa2["d1b"],
            "comb_d2b": Pa2["d2b"], "comb_ln2g": Pa2["ln2g"], "comb_ln2b": Pa2["ln2b"],
            "comb_ones1f": P["ones1f"], "comb_ones128b": P["ones128b"], "comb_cvec": P["cvec"],
            "qkv_WqT": WqT, "qkv_WkT": WkT, "qkv_WvT": WvT,
            "qkv_bq": pmaj(np.asarray(inputs["L1_attn_bq"], f32)),
            "qkv_bk": pmaj(np.asarray(inputs["L1_attn_bk"], f32)),
            "qkv_bv": pmaj(np.asarray(inputs["L1_attn_bv"], f32)),
            "lattn_WoT": WoT, "lattn_bo": pmaj(np.asarray(inputs["L1_attn_bo"], f32)),
            "lattn_hsel": hsel, "lattn_hselT": hselT, "lattn_cvec": P["cvec"],
            "lattn_masks": mask,
            "lffn_g8": Pl["g8"], "lffn_ones1f": P["ones1f"], "lffn_ones128b": P["ones128b"],
            "lffn_cvec": P["cvec"],
        }
        for i in range(6):
            m[f"lffn_qw{i}"] = Pl["qw"][i]
            m[f"lffn_sc{i}"] = Pl["sc"][i]
            m[f"lffn_bi{i}"] = Pl["bi"][i]
            m[f"lffn_la{i}"] = Pl["la"][i]
            m[f"lffn_lb{i}"] = Pl["lb"][i]
        for i in range(3):
            m[f"lffn_lg{i}"] = Pl["lg"][i]
            m[f"lffn_lbt{i}"] = Pl["lbt"][i]
        maps.append(m)
    return maps


def kernel(**inputs):
    ctx, F = build_fused()
    nc = ctx.finish()
    maps = fused_in_maps(inputs)
    res = run_bass_kernel_spmd(nc, maps, core_ids=list(range(8))).results
    out = np.empty((B, L, D), np.float32)
    for c in range(8):
        b, h = c // 2, c % 2
        out[b, h * 2048:(h + 1) * 2048, :] = np.asarray(res[c]["outh"], np.float32).T
    return out


def _run(nc, in_maps):
    return run_bass_kernel_spmd(nc, in_maps, core_ids=list(range(8))).results


def _c(a):
    return np.ascontiguousarray(a)


def kernel_unfused(**inputs):
    f32 = np.float32
    P = host_prep_pre(inputs)
    Ps = host_prep_scan(inputs)
    Pf = host_prep_ffn0(inputs)
    Pa1 = host_prep_attn1(inputs)
    Pa2 = host_prep_attn2(inputs)
    Pl = host_prep_lffn(inputs)
    ones1f = P["ones1f"]
    ones128b = P["ones128b"]
    CV = P["cvec"]

    nc = build_pre().finish()
    r1 = _run(nc, maps_pre(P))

    nc = build_scan().finish()
    r2 = _run(nc, maps_scan(Ps, r1))

    nc = build_ffn0().finish()
    r3 = _run(nc, maps_ffn0(P, Pf, r1))

    nc = build_attn1().finish()
    maps4 = []
    for c in range(8):
        b, h = c // 2, c % 2
        sl = slice(h * 2048, (h + 1) * 2048)
        maps4.append({
            "mp0": _c(np.asarray(r2[2 * b]["mp"])[:, sl]),
            "mp1": _c(np.asarray(r2[2 * b + 1]["mp"])[:, sl]),
            "xn0h": _c(np.asarray(r1[c]["xn0"])[:, sl]),
            "qwT": Pa1["qwT"], "kwT": Pa1["kwT"], "qb": Pa1["qb"], "kb": Pa1["kb"],
            "wg": Pa1["wg"], "ln0g": Pa1["ln0g"], "ln0b": Pa1["ln0b"],
            "ones1f": ones1f, "ones128b": ones128b, "cvec": CV,
        })
    r4 = _run(nc, maps4)

    nc = build_attn2().finish()
    maps5 = []
    for c in range(8):
        b, h = c // 2, c % 2
        maps5.append({
            "st0": np.asarray(r4[2 * b]["stats"]), "st1": np.asarray(r4[2 * b + 1]["stats"]),
            "h0i": np.asarray(r4[c]["h0"]), "k16i": np.asarray(r4[c]["k16"]),
            "projT": Pa2["projT"], "flT": Pa2["flT"], "pb": Pa2["pb"], "fb": Pa2["fb"],
            "lnig": Pa2["lnig"], "lnib": Pa2["lnib"], "ln1g": Pa2["ln1g"], "ln1b": Pa2["ln1b"],
            "ones1f": ones1f, "ones128b": ones128b, "cvec": CV,
        })
    r5 = _run(nc, maps5)

    nc = build_comb().finish()
    maps6 = []
    for c in range(8):
        b, h = c // 2, c % 2
        sl = slice(h * 2048, (h + 1) * 2048)
        maps6.append({
            "h2i": np.asarray(r5[c]["h2"]), "h2bi": np.asarray(r5[c]["h2b"]),
            "x0h": _c(P["x0T"][b][:, sl]),
            "g0i": np.asarray(r3[c]["g0"]),
            "d1T": Pa2["d1T"], "d2T": Pa2["d2T"], "d1b": Pa2["d1b"], "d2b": Pa2["d2b"],
            "ln2g": Pa2["ln2g"], "ln2b": Pa2["ln2b"],
            "ones1f": ones1f, "ones128b": ones128b, "cvec": CV,
        })
    r6 = _run(nc, maps6)

    nc = build_qkv().finish()
    maps7 = []
    WqT = Tc(np.asarray(inputs["L1_attn_Wq"], f32))
    WkT = Tc(np.asarray(inputs["L1_attn_Wk"], f32))
    WvT = Tc(np.asarray(inputs["L1_attn_Wv"], f32))
    bqv = pmaj(np.asarray(inputs["L1_attn_bq"], f32))
    bkv = pmaj(np.asarray(inputs["L1_attn_bk"], f32))
    bvv = pmaj(np.asarray(inputs["L1_attn_bv"], f32))
    for c in range(8):
        maps7.append({"xn1bi": np.asarray(r6[c]["xn1b"]), "WqT": WqT, "WkT": WkT,
                      "WvT": WvT, "bq": bqv, "bk": bkv, "bv": bvv})
    r7 = _run(nc, maps7)

    nc = build_lattn().finish()
    WoT = Tc(np.asarray(inputs["L1_attn_Wo"], f32))
    bov = pmaj(np.asarray(inputs["L1_attn_bo"], f32))
    hsel = np.zeros((128, 8 * 16), NB16)
    hselT = np.zeros((16, 8 * 128), f32)
    for ch in range(8):
        for p in range(128):
            j = 2 * ch + p // 64
            hsel[p, ch * 16 + j] = 1
            hselT[j, ch * 128 + p] = 1
    zeros2048 = np.zeros((D, 2048), NB16)
    maps8 = []
    for c in range(8):
        b, h = c // 2, c % 2
        kfirst = np.asarray(r7[2 * b]["k1"])
        ksec = np.asarray(r7[2 * b + 1]["k1"])
        vfirst = np.asarray(r7[2 * b]["v1"])
        vsec = np.asarray(r7[2 * b + 1]["v1"])
        if h == 0:
            Kx = _c(np.concatenate([zeros2048, kfirst], 1))
            Vx = _c(np.concatenate([zeros2048, vfirst], 1))
            Vo = vsec
        else:
            Kx = _c(np.concatenate([kfirst, ksec], 1))
            Vx = _c(np.concatenate([vfirst, vsec], 1))
            Vo = zeros2048
        maps8.append({"q16i": np.asarray(r7[c]["q1"]), "Kx": Kx, "Vx": Vx, "Vo": Vo,
                      "WoT": WoT, "bo": bov, "hsel": hsel, "hselT": hselT, "cvec": CV})
    r8 = _run(nc, maps8)

    nc = build_lffn().finish()
    maps9 = []
    for c in range(8):
        m = {"xn1i": np.asarray(r6[c]["xn1"]), "x1i": np.asarray(r6[c]["x1"]),
             "s1i": np.asarray(r8[c]["s1"]), "g8": Pl["g8"],
             "ones1f": ones1f, "ones128b": ones128b, "cvec": CV}
        for i in range(6):
            m[f"qw{i}"] = Pl["qw"][i]
            m[f"sc{i}"] = Pl["sc"][i]
            m[f"bi{i}"] = Pl["bi"][i]
            m[f"la{i}"] = Pl["la"][i]
            m[f"lb{i}"] = Pl["lb"][i]
        for i in range(3):
            m[f"lg{i}"] = Pl["lg"][i]
            m[f"lbt{i}"] = Pl["lbt"][i]
        maps9.append(m)
    r9 = _run(nc, maps9)

    out = np.empty((B, L, D), np.float32)
    for c in range(8):
        b, h = c // 2, c % 2
        out[b, h * 2048:(h + 1) * 2048, :] = np.asarray(r9[c]["outh"], np.float32).T
    return out
